# version 05 (final)
# baseline (speedup 1.0000x reference)
import numpy as np
import ml_dtypes
from contextlib import ExitStack

import concourse.bass as bass
import concourse.mybir as mybir
from concourse.bass_utils import run_bass_kernel_spmd
from concourse.bass_core_rust import IndirectOffsetOnAxis

F32 = mybir.dt.float32
BF16 = mybir.dt.bfloat16
I32 = mybir.dt.int32
AF = mybir.ActivationFunctionType
OP = mybir.AluOpType
AX = mybir.AxisListType.X
BF = ml_dtypes.bfloat16

NCORE = 8
SEQ = 4
T = 2048
C = 1024
FF = 4096
TB = 512
NB = SEQ * T // TB
NCT = C // 128
NFC = FF // 128
EPS = 1e-5
LIMIT_BLOCKS = None
START_BLOCK = 0
SERIAL_DEBUG = False
A_STEP = 99
A_SUB = 'full'

O_W1 = 0
O_W2 = 65536
O_X = 131072
O_COMB = 147456
O_T32 = 163840
O_XBF = 180224
O_XSQ = 188416
ARENA_END = 196608
PO = ARENA_END
RESERVE = 224896


class Prog:

    ENGS = ("tensor", "vector", "scalar", "gpsimd", "sync")

    def __init__(self):
        self.recs = {e: [] for e in self.ENGS}
        self.count = {e: 0 for e in self.ENGS}
        self.pending_barrier = {e: None for e in self.ENGS}

    def op(self, eng, method, *args, waits=(), dma=None, **kwargs):
        key = eng if dma is None else dma
        inc = 1 if dma is None else 16
        if key not in self.count:
            self.count[key] = 0
        ws = {}
        if SERIAL_DEBUG:
            for k2, v in self.count.items():
                if k2 != eng and v > 0:
                    ws[k2] = v
        if self.pending_barrier[eng] is not None:
            for k2, v in self.pending_barrier[eng].items():
                if k2 != eng and v > 0:
                    ws[k2] = max(ws.get(k2, 0), v)
            self.pending_barrier[eng] = None
        for w in waits:
            if w is None:
                continue
            k2, v = w
            ws[k2] = max(ws.get(k2, 0), v)
        self.count[key] += inc
        self.recs[eng].append((method, args, kwargs, ws, inc, key))
        return (key, self.count[key])

    def mms(self, mmlist, waits=()):
        return self.op("tensor", "__mmgroup__", mmlist, waits=waits)

    def barrier(self):
        snap = dict(self.count)
        for e in self.ENGS:
            self.pending_barrier[e] = snap

    def emit(self, nc, block, es):
        keys = list(self.count.keys())
        sems = {k: es.enter_context(nc.semaphore("sem_" + k)) for k in keys}
        for ename in self.ENGS:
            recs = self.recs[ename]
            cnt = self.count

            def body(eng, recs=recs, ename=ename):
                waited = {k: 0 for k in keys}
                for method, args, kwargs, ws, inc, key in recs:
                    for k2, v in ws.items():
                        if v > waited[k2]:
                            eng.wait_ge(sems[k2], v)
                            waited[k2] = v
                    if method == "__mmgroup__":
                        inst = None
                        for margs, mkw in args[0]:
                            inst = eng.matmul(*margs, **mkw)
                    else:
                        inst = getattr(eng, method)(*args, **kwargs)
                    inst.then_inc(sems[key], inc)
                if ename == "gpsimd":
                    for k in keys:
                        if cnt[k] > 0:
                            eng.wait_ge(sems[k], cnt[k])

            getattr(block, ename)(body)


def build_program(dbg=(), phases=("a", "b", "c", "e1", "e2", "f")):
    nc = bass.Bass("TRN2", target_bir_lowering=False,
                   dynamic_dma_scratch_size=4096)

    def dram_in(name, shape, dtype=F32):
        return nc.dram_tensor(name, shape, dtype, kind="ExternalInput").ap()

    def dram_out(name, shape, dtype=F32):
        return nc.dram_tensor(name, shape, dtype, kind="ExternalOutput").ap()

    def dram_int(name, shape, dtype=F32):
        kind = "ExternalOutput" if name in dbg else "Internal"
        return nc.dram_tensor(name, shape, dtype, kind=kind).ap()

    xT = dram_in("xT", [SEQ, C, T])
    wiT = dram_in("wiT", [C, FF])
    woT = dram_in("woT", [FF, C])
    wrT = dram_in("wrT", [C, C])
    wkT = dram_in("wkT", [C, C])
    wvT = dram_in("wvT", [C, C])
    woaT = dram_in("woaT", [C, C])
    wfkT = dram_in("wfkT", [C, FF])
    wfvT = dram_in("wfvT", [FF, C])
    wfrT = dram_in("wfrT", [C, C])
    AsT = dram_in("AsT", [8, C * 16])
    BsT = dram_in("BsT", [32, 16384])
    ln1w = dram_in("ln1w", [128, 8])
    ln1b = dram_in("ln1b", [128, 8])
    ln2w = dram_in("ln2w", [128, 8])
    ln2b = dram_in("ln2b", [128, 8])
    dec = dram_in("dec", [128, 8])
    uu = dram_in("uu", [128, 8])
    tki = dram_in("tki", [2, 1], I32)
    tkv = dram_in("tkv", [1, 2])
    onesf_d = dram_in("onesf", [128, 1])
    onesb_d = dram_in("onesb", [128, 1], BF16)
    onesr_d = dram_in("onesr", [65, 128])

    outT = dram_out("outT", [SEQ, C, T])

    h1_d = dram_int("h1", [SEQ, C, T])
    xa_d = dram_int("xa", [SEQ, C, T])
    k_d = dram_int("kd", [SEQ, C, T])
    v_d = dram_int("vd", [SEQ, C, T])
    r_d = dram_int("rd", [SEQ, C, T], BF16)
    xm_d = dram_int("xm", [SEQ, C, T])
    kk_d = dram_int("kk", [SEQ, FF, T], BF16)
    sr_d = dram_int("sr", [SEQ, C, T], BF16)
    h2_d = dram_int("h2", [SEQ, C, T])
    rows_a_d = dram_int("rows_a", [NB, 3, TB])
    rows_2_d = dram_int("rows_2", [NB, 2, TB])
    rows_f_d = dram_int("rows_f", [NB, 2, TB])

    es = ExitStack()
    with es:
        BASE, _end = nc.bump_sbuf(RESERVE)

        def at(name, off, shape, dtype):
            return nc.alloc_sbuf_tensor_at(name, shape, dtype, offset=BASE + off)[:]

        pcur = [PO]

        def pers(name, shape, dtype):
            t = at(name, pcur[0], shape, dtype)
            sz = int(np.prod(shape[1:])) * mybir.dt.size(dtype)
            pcur[0] += (sz + 31) // 32 * 32
            return t

        tmpc = pers("tmpc", [128, TB], F32)
        tmpc_bf = pers("tmpc_bf", [128, TB], BF16)
        rowf = pers("rowf", [1, 6 * TB], F32)
        R_MU = rowf[:, 0:TB]
        R_VAR = rowf[:, TB:2 * TB]
        R_RSD = rowf[:, 2 * TB:3 * TB]
        R_MURS = rowf[:, 3 * TB:4 * TB]
        R_RS2 = rowf[:, 4 * TB:5 * TB]
        R_TMP = rowf[:, 5 * TB:6 * TB]
        ln1w8 = pers("ln1w8", [128, 8], F32)
        ln1b8 = pers("ln1b8", [128, 8], F32)
        ln2w8 = pers("ln2w8", [128, 8], F32)
        ln2b8 = pers("ln2b8", [128, 8], F32)
        dec8 = pers("dec8", [128, 8], F32)
        u8 = pers("u8", [128, 8], F32)
        ew8 = pers("ew8", [128, 8], F32)
        eu8 = pers("eu8", [128, 8], F32)
        cval = pers("cval", [128, 2], F32)
        onesf = pers("onesf_s", [128, 1], F32)
        onesb = pers("onesb_s", [128, 1], BF16)
        onesr = pers("onesr_s", [65, 128], F32)
        vt = pers("vt_s", [1, 2], F32)
        idx_sb = pers("idx_s", [2, 1], I32)
        idx4 = pers("idx4_s", [2, 1], I32)
        bkv = pers("bkv", [128, 24], F32)
        brh8 = pers("brh8", [128, 8], F32)
        bfk32 = pers("bfk32", [128, 32], F32)
        bfrh8 = pers("bfrh8", [128, 8], F32)
        A_sb = pers("A_sb", [128, 256], BF16)
        midcat = pers("midcat", [32, TB], BF16)
        mid0t = pers("mid0t", [16, TB], BF16)
        Bcat = pers("Bcat", [32, FF], BF16)
        Bsel0 = at("Bsel0", O_XSQ, [16, FF], BF16)
        assert pcur[0] <= RESERVE, f"persistent overflow {pcur[0]}"

        ps = [es.enter_context(nc.psum_tensor(f"ps{i}", [128, TB], F32))[:]
              for i in range(8)]

        prog = Prog()
        PP = prog.op

        def arena(name, off, shape, dtype):
            return at(name, off, shape, dtype)

        t_ld = {}
        for name, dst, src in [
            ("ln1w", ln1w8, ln1w), ("ln1b", ln1b8, ln1b),
            ("ln2w", ln2w8, ln2w), ("ln2b", ln2b8, ln2b),
            ("dec", dec8, dec), ("u", u8, uu),
            ("onesf", onesf, onesf_d), ("onesb", onesb, onesb_d),
            ("onesr", onesr, onesr_d), ("vt", vt, tkv), ("idx", idx_sb, tki),
        ]:
            t_ld[name] = PP("sync", "dma_start", out=dst, in_=src, dma="sv")
        tok_vecs = t_ld["idx"]

        tok_ed = PP("scalar", "activation", ew8, dec8, AF.Exp, waits=[tok_vecs])
        tok_ew = PP("scalar", "activation", ew8, ew8, AF.Exp, scale=-1.0, waits=[tok_ed])
        tok_eu = PP("scalar", "activation", eu8, u8, AF.Exp, waits=[tok_vecs])

        tok_cv1 = prog.mms([((ps[0][:, 0:2], onesr[0:1, :], vt),
                            dict(start=True, stop=True))],
                           waits=[tok_vecs])
        tok_cval = PP("vector", "tensor_copy", cval, ps[0][:, 0:2], waits=[tok_cv1])

        gtmp = arena("gtmp", O_W1, [2, 16384], F32)
        AstF = arena("AstF", O_T32, [128, 256], F32)
        BstF = arena("BstF", O_X, [48, FF], F32)

        tok_gA = PP("gpsimd", "indirect_dma_start", gtmp[:], None, AsT[:, :],
                    IndirectOffsetOnAxis(idx_sb, 0), waits=[tok_vecs], dma="g0")
        tA = None
        for ei in range(2):
            for ci in range(NCT):
                tA = PP("gpsimd", "dma_start",
                        out=AstF[:, (ei * 8 + ci) * 16:(ei * 8 + ci) * 16 + 16],
                        in_=gtmp[ei:ei + 1, ci * 2048:(ci + 1) * 2048],
                        waits=[tok_gA], dma="g0")
        tok_Acast = PP("vector", "tensor_copy", A_sb, AstF, waits=[tA])

        tok_moves = [tA]
        tok_i4 = None
        for ch in range(4):
            tok_i4 = PP("vector", "tensor_scalar", idx4, idx_sb, 4, ch,
                        OP.mult, OP.add, waits=[tok_vecs, tok_i4, tok_moves[-1]])
            tok_g = PP("gpsimd", "indirect_dma_start", gtmp[:], None,
                       BsT[:, :], IndirectOffsetOnAxis(idx4, 0),
                       waits=[tok_i4] + tok_moves[-2:], dma="g0")
            tok_moves.append(PP("gpsimd", "dma_start",
                                out=BstF[4 * ch:4 * ch + 4, :],
                                in_=gtmp[0:1, :], waits=[tok_g], dma="g0"))
            tok_moves.append(PP("gpsimd", "dma_start",
                                out=BstF[32 + 4 * ch:32 + 4 * ch + 4, :],
                                in_=gtmp[1:2, :], waits=[tok_g], dma="g0"))
        negtmp = arena("negtmp", O_COMB, [16, FF], BF16)
        tokB0 = PP("vector", "tensor_copy", Bsel0[0:16, :], BstF[0:16, :],
                   waits=tok_moves[1:])
        tokB1 = PP("vector", "tensor_copy", Bcat[0:16, :], BstF[32:48, :],
                   waits=tok_moves[1:] + [tokB0])
        tokBn = PP("vector", "tensor_scalar_mul", negtmp[0:16, :], BstF[0:16, :],
                   -1.0, waits=[tokB1])
        tok_Bdone = PP("gpsimd", "dma_start", out=Bcat[16:32, :],
                       in_=negtmp[0:16, :], waits=[tokBn], dma="g0")

        prog.barrier()

        def weight_load_cast(wdram, nchunk, chunk_cols, dst, tag, fold=None,
                             bvec=None, bk_nf=0, guard=None, bias_dst=None,
                             bias_col0=0, bias_scale=None, brow=None):
            stg = [arena(f"stg_{tag}_0", O_COMB, [128, chunk_cols], F32),
                   arena(f"stg_{tag}_1", O_T32, [128, chunk_cols], F32)]
            toks_cast = [guard, guard]
            toks_bk = [None, None]
            nsl = chunk_cols // TB
            for ci in range(nchunk):
                par = ci % 2
                tok_l = PP("sync", "dma_start", out=stg[par][:],
                           in_=wdram[ci * 128:(ci + 1) * 128, :],
                           waits=[toks_cast[par], toks_bk[par]],
                           dma="w%d" % par)
                if fold is not None:
                    toks_cast[par] = PP(
                        "vector", "tensor_scalar_mul",
                        dst[:, ci * chunk_cols:(ci + 1) * chunk_cols], stg[par][:],
                        fold[:, ci:ci + 1], waits=[tok_l])
                else:
                    toks_cast[par] = PP(
                        "vector", "tensor_copy",
                        dst[:, ci * chunk_cols:(ci + 1) * chunk_cols], stg[par][:],
                        waits=[tok_l])
                if bvec is not None:
                    mm = [((ps[sl][0:1, :], bvec[:, ci:ci + 1],
                            stg[par][:, sl * TB:(sl + 1) * TB]),
                           dict(start=(ci == 0), stop=(ci == nchunk - 1),
                                skip_group_check=True))
                          for sl in range(nsl)]
                    toks_bk[par] = prog.mms(mm, waits=[tok_l])
            tok_bias = None
            if bvec is not None:
                tok_mm = toks_bk[(nchunk - 1) % 2]
                t_ev = None
                for sl in range(nsl):
                    t_ev = PP("vector", "tensor_copy",
                              brow[0:1, sl * TB:(sl + 1) * TB], ps[sl][0:1, :],
                              waits=[tok_mm, t_ev])
                for fj in range(bk_nf):
                    tok_bias = PP("scalar", "dma_start",
                                  out=bias_dst[:, bias_col0 + fj:bias_col0 + fj + 1],
                                  in_=brow[0:1, fj * 128:fj * 128 + 128],
                                  waits=[t_ev], dma="sca")
                if bias_scale is not None:
                    tok_bias = PP("vector", "tensor_scalar_mul",
                                  bias_dst[:, bias_col0:bias_col0 + bk_nf],
                                  bias_dst[:, bias_col0:bias_col0 + bk_nf],
                                  bias_scale, waits=[tok_bias])
            return toks_cast, tok_bias

        def stats_rows(x8, xsq, tok_x, rs2=False, extra_w=()):
            w0 = [tok_x] + list(extra_w)
            tok_sq = PP("scalar", "activation", xsq, x8, AF.Square, waits=w0)
            mmA = [((ps[0][0:1, :], onesf, x8[:, ci * TB:(ci + 1) * TB]),
                    dict(start=(ci == 0), stop=(ci == NCT - 1),
                         skip_group_check=True)) for ci in range(NCT)]
            tok_s1 = prog.mms(mmA, waits=w0)
            mmB = [((ps[1][0:1, :], onesb, xsq[:, ci * TB:(ci + 1) * TB]),
                    dict(start=(ci == 0), stop=(ci == NCT - 1),
                         skip_group_check=True)) for ci in range(NCT)]
            tok_s2 = prog.mms(mmB, waits=[tok_sq])
            t1 = PP("vector", "tensor_scalar_mul", R_MU, ps[0][0:1, :], 1.0 / C,
                    waits=[tok_s1])
            t2 = PP("vector", "tensor_scalar_mul", R_VAR, ps[1][0:1, :], 1.0 / C,
                    waits=[tok_s2])
            t3 = PP("vector", "tensor_tensor", R_TMP, R_MU, R_MU, OP.mult,
                    waits=[t1])
            t4 = PP("vector", "tensor_tensor", R_VAR, R_VAR, R_TMP, OP.subtract,
                    waits=[t2, t3])
            t4b = PP("vector", "tensor_scalar_add", R_TMP, R_VAR, EPS, waits=[t4])
            t5 = PP("scalar", "activation", R_TMP, R_TMP, AF.Ln, waits=[t4b])
            t6 = PP("scalar", "activation", R_RSD, R_TMP, AF.Exp, scale=-0.5,
                    waits=[t5])
            t7 = PP("vector", "tensor_tensor", R_MURS, R_MU, R_RSD, OP.mult,
                    waits=[t6])
            out = [t6, t7, None, tok_sq, tok_s2]
            if rs2:
                u1 = PP("vector", "tensor_scalar_add", R_TMP, R_VAR, EPS,
                        waits=[t4, t5, t6])
                u2 = PP("vector", "reciprocal", R_RS2, R_TMP, waits=[u1])
                u3 = PP("vector", "tensor_tensor", R_RS2, R_VAR, R_RS2, OP.mult,
                        waits=[u2])
                u3b = PP("vector", "tensor_scalar_add", R_TMP, R_RS2, EPS,
                         waits=[u3])
                u4 = PP("scalar", "activation", R_TMP, R_TMP, AF.Ln,
                        waits=[u3b, t6])
                u5 = PP("scalar", "activation", R_RS2, R_TMP, AF.Exp, scale=-0.5,
                        waits=[u4])
                out[2] = u5
            return out

        def bcast(row_slice, bank, waits):
            b0 = row_slice.base_partition()
            return prog.mms([((ps[bank][:], onesr[b0:b0 + 1, :], row_slice),
                              dict(start=True, stop=True))], waits=waits)

        def xn_from_bcast(x8, t32, xnbf, op2, tok_b1, tok_b2, tok_x):
            x3 = x8.rearrange("p (a b) -> p a b", a=NCT)
            t3v = t32.rearrange("p (a b) -> p a b", a=NCT)
            xn3 = xnbf.rearrange("p (a b) -> p a b", a=NCT)
            bc1 = ps[1][:].unsqueeze(1).broadcast_to([128, NCT, TB])
            bc2 = ps[2][:].unsqueeze(1).broadcast_to([128, NCT, TB])
            ta = PP("vector", "tensor_tensor", t3v, x3, bc1, OP.mult,
                    waits=[tok_b1, tok_x])
            tb = PP("vector", "tensor_tensor", xn3, t3v, bc2, op2,
                    waits=[tok_b2, ta])
            return ta, tb

        def phase_ffn(src_d, dst_d, in_compute, rows_out_d, rows_in_d, tag):
            wi_sb = arena(f"wi_{tag}", O_W1, [128, 8 * FF], BF16)
            wo_sb = arena(f"wo_{tag}", O_W2, [128, 32 * C], BF16)
            x8 = arena(f"x8_{tag}", O_X, [128, NCT * TB], F32)
            comb = arena(f"comb_{tag}", O_COMB, [128, 16 * TB], BF16)
            t32 = arena(f"t32_{tag}", O_T32, [128, NCT * TB], F32)
            xnbf = arena(f"xn_{tag}", O_XBF, [128, NCT * TB], BF16)
            xsq = arena(f"xsq_{tag}", O_T32, [128, NCT * TB], BF16)

            if tag == "f":
                dF = PP("gpsimd", "dma_start", out=Bsel0[0:16, :],
                        in_=Bcat[16:32, :], dma="g0")
                tok_bs0 = PP("vector", "tensor_scalar_mul", Bsel0[0:16, :],
                             Bsel0[0:16, :], -1.0, waits=[dF])
            else:
                tok_bs0 = tok_Bdone
            tok_wi, _ = weight_load_cast(wiT, NCT, FF, wi_sb, "wi" + tag)
            tok_wo, _ = weight_load_cast(woT, NFC, C, wo_sb, "wo" + tag,
                                         guard=tok_wi[1])

            tok_store_last = None
            tok_xfree = None
            tok_rows_dma = None
            tok_midpk_prev = None
            tok_u2_last = None
            tok_combprev = None
            for b in range(START_BLOCK, LIMIT_BLOCKS or NB):
                s, t0 = b // 4, (b % 4) * TB
                tok_ld = None
                for ci in range(NCT):
                    w0b = [tok_store_last, tok_xfree]
                    if b == 0:
                        w0b += [tok_wi[0], tok_wi[1], tok_wo[0], tok_wo[1],
                                tok_Bdone, tok_bs0]
                    tok_ld = PP("sync", "dma_start",
                                out=x8[:, ci * TB:(ci + 1) * TB],
                                in_=src_d[s, ci * 128:(ci + 1) * 128, t0:t0 + TB],
                                waits=w0b, dma="sx")
                if A_STEP < 2:
                    tok_xfree = tok_ld
                    continue
                if in_compute:
                    trs = stats_rows(x8, xsq, tok_ld, extra_w=[tok_rows_dma])
                    tok_rsd, tok_murs = trs[0], trs[1]
                    row_rsd, row_murs = R_RSD, R_MURS
                    if "rows_2" in dbg:
                        PP("scalar", "dma_start", out=rows_2_d[b, 0:1, :],
                           in_=R_RSD, waits=[trs[0]], dma="grow")
                        PP("scalar", "dma_start", out=rows_2_d[b, 1:2, :],
                           in_=R_MURS, waits=[trs[1]], dma="grow")
                else:
                    tr2 = PP("sync", "dma_start", out=rowf[:, 0:2 * TB],
                             in_=rows_in_d[b, :, :], waits=[tok_xfree],
                             dma="srow")
                    tok_rsd = tok_murs = tr2
                    row_rsd, row_murs = rowf[:, 0:TB], rowf[:, TB:2 * TB]
                if A_STEP < 3:
                    tok_xfree = trs[-1]
                    continue
                tok_b1 = bcast(row_rsd, 1, [tok_rsd])
                tok_b2 = bcast(row_murs, 2, [tok_murs])
                tok_t1, tok_xn = xn_from_bcast(x8, t32, xnbf, OP.subtract,
                                               tok_b1, tok_b2, tok_ld)
                if "kk" in dbg:
                    for ci in range(NCT):
                        PP("scalar", "dma_start",
                           out=kk_d[s, (b % 4) * NCT * 128 + ci * 128:
                                    (b % 4) * NCT * 128 + ci * 128 + 128,
                                    t0:t0 + TB][0:128 - 0, :]
                           if False else kk_d[s, ci * 128:(ci + 1) * 128, t0:t0 + TB],
                           in_=xnbf[:, ci * TB:(ci + 1) * TB],
                           waits=[tok_xn], dma="grow")
                if A_STEP < 4:
                    tok_xfree = tok_xn
                    continue
                mmM = []
                for ei in range(2):
                    for ci in range(NCT):
                        mmM.append(((ps[3][ei * 32:ei * 32 + 16, :],
                                     A_sb[:, (ei * 8 + ci) * 16:(ei * 8 + ci) * 16 + 16],
                                     xnbf[:, ci * TB:(ci + 1) * TB]),
                                    dict(start=(ci == 0), stop=(ci == NCT - 1),
                                         skip_group_check=True)))
                tok_mid = prog.mms(mmM, waits=[tok_xn, tok_Acast])
                tok_midb0 = PP("vector", "tensor_copy", mid0t[0:16, :],
                               ps[3][0:16, :],
                               waits=[tok_mid, tok_midpk_prev])
                tok_midb1 = PP("vector", "tensor_copy", midcat[0:16, :],
                               ps[3][32:48, :], waits=[tok_mid, tok_midb0])
                tok_midpk = PP("scalar", "dma_start", out=midcat[16:32, :],
                               in_=mid0t[0:16, :],
                               waits=[tok_midb0, tok_u2_last], dma="gmid")
                tok_midbf = tok_midpk
                tok_midpk_prev = tok_midpk

                if A_STEP < 5:
                    tok_xfree = tok_midbf
                    continue
                tok_comb = [None] * 16
                tok_h1 = [None, None]
                for half in range(2):
                    for j in range(16):
                        fi = half * 16 + j
                        bank = 4 + (j % 2)
                        mmU = [((ps[bank][:],
                                 wi_sb[:, ci * FF + fi * 128:ci * FF + fi * 128 + 128],
                                 xnbf[:, ci * TB:(ci + 1) * TB]),
                                dict(start=(ci == 0), stop=False,
                                     skip_group_check=True))
                               for ci in range(NCT)]
                        mmU.append(((ps[bank][:], Bsel0[0:16, fi * 128:fi * 128 + 128],
                                     mid0t[0:16, :]),
                                    dict(start=False, stop=True,
                                         skip_group_check=True)))
                        tok_U = prog.mms(mmU, waits=[tok_midbf, tok_midb1,
                                                     tok_comb[j - 2] if j >= 2
                                                     else tok_combprev])
                        if A_SUB == "u":
                            tok_comb[j] = PP("vector", "tensor_copy",
                                             comb[:, j * TB:(j + 1) * TB],
                                             ps[bank][:], waits=[tok_U])
                            continue
                        tok_e0 = PP("scalar", "activation",
                                    comb[:, j * TB:(j + 1) * TB], ps[bank][:],
                                    AF.Relu, scale=cval[:, 0:1],
                                    waits=[tok_U, tok_cval])
                        if A_SUB == "e0":
                            tok_comb[j] = tok_e0
                            continue
                        tok_U2 = prog.mms(
                            [((ps[bank][:], Bcat[0:32, fi * 128:fi * 128 + 128],
                               midcat[0:32, :]),
                              dict(start=False, stop=True,
                                   skip_group_check=True))], waits=[tok_e0])
                        tok_u2_last = tok_U2
                        if A_SUB == "u2s":
                            tok_comb[j] = PP("vector", "tensor_copy",
                                             comb[:, j * TB:(j + 1) * TB],
                                             ps[bank][:], waits=[tok_U2, tok_e0])
                            continue
                        tok_e1 = PP("vector", "tensor_scalar", tmpc_bf, ps[bank][:],
                                    0.0, cval[:, 1:2], OP.max, OP.mult,
                                    waits=[tok_U2, tok_comb[j - 1] if j >= 1
                                           else (tok_comb[15] if half == 1
                                                 else None)])
                        if A_SUB == "e1":
                            tok_comb[j] = tok_e1
                            continue
                        tok_comb[j] = PP("vector", "tensor_tensor",
                                         comb[:, j * TB:(j + 1) * TB],
                                         comb[:, j * TB:(j + 1) * TB], tmpc_bf,
                                         OP.add, waits=[tok_e1, tok_e0])
                    if A_STEP < 6:
                        continue
                    for cj in range(NCT):
                        bank = 6 + (cj % 2)
                        mmD = [((ps[bank][:],
                                 wo_sb[:, (half * 16 + fj) * C + cj * 128:
                                       (half * 16 + fj) * C + cj * 128 + 128],
                                 comb[:, fj * TB:(fj + 1) * TB]),
                                dict(start=(fj == 0), stop=(fj == 15),
                                     skip_group_check=True))
                               for fj in range(16)]
                        tok_D = prog.mms(mmD, waits=[tok_comb[15],
                                                     tok_h1[cj % 2]])
                        tok_h1[cj % 2] = PP("vector", "scalar_tensor_tensor",
                                            x8[:, cj * TB:(cj + 1) * TB],
                                            ps[bank][:], 0.5,
                                            x8[:, cj * TB:(cj + 1) * TB],
                                            OP.mult, OP.add, waits=[tok_D, tok_t1])
                        if half == 1:
                            tok_store_last = PP("scalar", "dma_start",
                                out=dst_d[s, cj * 128:(cj + 1) * 128, t0:t0 + TB],
                                in_=x8[:, cj * TB:(cj + 1) * TB],
                                waits=[tok_h1[cj % 2]], dma="go")
                if A_STEP < 6:
                    tok_xfree = tok_comb[15]
                    continue
                if A_STEP < 7:
                    tok_xfree = tok_h1[1]
                    continue
                tok_combprev = tok_comb[15]
                if rows_out_d is not None:
                    trs = stats_rows(x8, xsq, tok_h1[0], rs2=True,
                                     extra_w=[tok_h1[1], tok_rows_dma])
                    d1 = PP("scalar", "dma_start", out=rows_out_d[b, 0:1, :],
                            in_=R_RSD, waits=[trs[0]], dma="grow")
                    d2 = PP("scalar", "dma_start", out=rows_out_d[b, 1:2, :],
                            in_=R_MURS, waits=[trs[1]], dma="grow")
                    d3 = PP("scalar", "dma_start", out=rows_out_d[b, 2:3, :],
                            in_=R_RS2, waits=[trs[2]], dma="grow")
                    tok_rows_dma = d3
                    tok_xfree = trs[4]
                else:
                    tok_xfree = tok_h1[1]

        def phase_b():
            wr_sb = arena("wr_b", O_W1, [128, 8 * C], BF16)
            wk_sb = arena("wk_b", O_W1 + 16384, [128, 8 * C], BF16)
            wv_sb = arena("wv_b", O_W1 + 32768, [128, 8 * C], BF16)
            kbuf = arena("kbuf_b", O_W2, [128, NCT * TB], F32)
            vbuf = arena("vbuf_b", O_W2 + 16384, [128, NCT * TB], F32)
            rbuf = arena("rbuf_b", O_W2 + 32768, [128, NCT * TB], BF16)
            x8 = arena("x8_b", O_X, [128, NCT * TB], F32)
            t32 = arena("t32_b", O_T32, [128, NCT * TB], F32)
            xnbf = arena("xn_b", O_XBF, [128, NCT * TB], BF16)

            brow = arena("brow_b", O_XBF, [1, C], F32)
            _, tok_bkr = weight_load_cast(wrT, NCT, C, wr_sb, "wr", fold=ln1w8,
                                          bvec=ln1b8, bk_nf=8,
                                          bias_dst=bkv, bias_col0=0, brow=brow)
            teh = PP("vector", "tensor_scalar_mul", brh8, bkv[:, 0:8], 0.5,
                     waits=[tok_bkr])
            _, tok_bkk = weight_load_cast(wkT, NCT, C, wk_sb, "wk", fold=ln1w8,
                                          bvec=ln1b8, bk_nf=8, guard=teh,
                                          bias_dst=bkv, bias_col0=8, brow=brow)
            _, te3 = weight_load_cast(wvT, NCT, C, wv_sb, "wv", fold=ln1w8,
                                      bvec=ln1b8, bk_nf=8, guard=tok_bkk,
                                      bias_dst=bkv, bias_col0=16, brow=brow)
            if "h2" in dbg:
                PP("scalar", "dma_start", out=h2_d[0, 0:128, 0:24],
                   in_=bkv[:, 0:24], waits=[te3], dma="grow")
                PP("scalar", "dma_start", out=h2_d[0, 0:128, 24:32],
                   in_=brh8[:, 0:8], waits=[te3], dma="grow")

            tok_st = {"r": None, "k": None, "v": None, "xa": None}
            tok_xn_prev = None
            for b in range(NB):
                s, t0 = b // 4, (b % 4) * TB
                tok_ld = None
                for ci in range(NCT):
                    tok_ld = PP("sync", "dma_start",
                                out=x8[:, ci * TB:(ci + 1) * TB],
                                in_=h1_d[s, ci * 128:(ci + 1) * 128, t0:t0 + TB],
                                waits=[tok_xn_prev,
                                       te3 if b == 0 else None], dma="sx")
                tr = PP("sync", "dma_start", out=rowf[:, 0:3 * TB],
                        in_=rows_a_d[b, :, :], waits=[tok_xn_prev], dma="srow")
                tok_b1 = bcast(rowf[:, 0:TB], 1, [tr])
                tok_b2 = bcast(rowf[:, TB:2 * TB], 2, [tr])
                tok_b3 = bcast(rowf[:, 2 * TB:3 * TB], 3, [tr, te3])
                x3 = x8.rearrange("p (a b) -> p a b", a=NCT)
                t3v = t32.rearrange("p (a b) -> p a b", a=NCT)
                xn3 = xnbf.rearrange("p (a b) -> p a b", a=NCT)
                bc1 = ps[1][:].unsqueeze(1).broadcast_to([128, NCT, TB])
                bc2 = ps[2][:].unsqueeze(1).broadcast_to([128, NCT, TB])
                bc3 = ps[3][:].unsqueeze(1).broadcast_to([128, NCT, TB])
                ta = PP("vector", "tensor_tensor", t3v, x3, bc1, OP.mult,
                        waits=[tok_b1, tok_ld, tok_st["xa"]])
                tok_xa = PP("vector", "tensor_tensor", t3v, t3v, bc2, OP.subtract,
                            waits=[tok_b2, ta])
                for ci in range(NCT):
                    tok_st["xa"] = PP("scalar", "dma_start",
                        out=xa_d[s, ci * 128:(ci + 1) * 128, t0:t0 + TB],
                        in_=t32[:, ci * TB:(ci + 1) * TB],
                        waits=[tok_xa], dma="sca")
                tok_xn = PP("vector", "tensor_tensor", xn3, t3v, bc3, OP.mult,
                            waits=[tok_b3, tok_xa])
                tok_xn_prev = tok_xn
                if "kk" in dbg:
                    for ci in range(NCT):
                        PP("scalar", "dma_start",
                           out=kk_d[s, ci * 128:(ci + 1) * 128, t0:t0 + TB],
                           in_=xnbf[:, ci * TB:(ci + 1) * TB],
                           waits=[tok_xn], dma="grow")
                tok_evprev = None
                for pi in range(3):
                    name = ["r", "k", "v"][pi]
                    wsb = [wr_sb, wk_sb, wv_sb][pi]
                    for cj in range(NCT):
                        bank = 4 + (cj % 2)
                        mmP = [((ps[bank][:],
                                 wsb[:, ci * C + cj * 128:ci * C + cj * 128 + 128],
                                 xnbf[:, ci * TB:(ci + 1) * TB]),
                                dict(start=(ci == 0), stop=(ci == NCT - 1),
                                     skip_group_check=True))
                               for ci in range(NCT)]
                        tok_P = prog.mms(mmP, waits=[tok_xn, tok_st[name],
                                                     tok_evprev])
                        if pi == 0:
                            tt = PP("scalar", "activation", tmpc, ps[bank][:],
                                    AF.Tanh, scale=0.5, bias=brh8[:, cj:cj + 1],
                                    waits=[tok_P, tok_evprev])
                            tok_ev = PP("vector", "tensor_scalar",
                                        rbuf[:, cj * TB:(cj + 1) * TB], tmpc,
                                        0.5, 0.5, OP.mult, OP.add, waits=[tt])
                            tok_evprev = tok_ev
                            buf, dd, q = rbuf, r_d, "scalar"
                        else:
                            buf = kbuf if pi == 1 else vbuf
                            dd = k_d if pi == 1 else v_d
                            q = "scalar"
                            tok_ev = PP("vector", "tensor_scalar_add",
                                        buf[:, cj * TB:(cj + 1) * TB], ps[bank][:],
                                        bkv[:, 8 * pi + cj:8 * pi + cj + 1],
                                        waits=[tok_P])
                            tok_evprev = tok_ev
                        tok_st[name] = PP(q, "dma_start",
                            out=dd[s, cj * 128:(cj + 1) * 128, t0:t0 + TB],
                            in_=buf[:, cj * TB:(cj + 1) * TB],
                            waits=[tok_ev],
                            dma={"r": "scr", "k": "gk", "v": "gv"}[name])

        def phase_c():
            woa_sb = arena("woa_c", O_W1, [128, 8 * C], BF16)
            k2 = [arena("k2a", O_W1 + 16384, [128, T], F32),
                  arena("k2b", O_W1 + 24576, [128, T], F32)]
            v2 = [arena("v2a", O_W1 + 32768, [128, T], F32),
                  arena("v2b", O_W1 + 40960, [128, T], F32)]
            r2 = [arena("r2a", O_W1 + 49152, [128, T], BF16),
                  arena("r2b", O_W1 + 53248, [128, T], BF16)]
            xsqc = arena("xsq_c", O_W1 + 57344, [128, NCT * TB], BF16)
            kap = arena("kap_c", O_W2, [128, T], F32)
            d1a = arena("d1a_c", O_W2 + 8192, [128, T], F32)
            Afb = arena("Af_c", O_W2 + 16384, [128, T + 4], F32)
            Bfb = arena("Bf_c", O_W2 + 24832, [128, T + 4], F32)
            Abb = arena("Ab_c", O_W2 + 33280, [128, T + 4], F32)
            Bbb = arena("Bb_c", O_W2 + 41728, [128, T + 4], F32)
            rfb = arena("rf_c", O_W2 + 50176, [128, T], F32)
            negM = arena("negM_c", O_W2 + 58368, [128, 1], F32)
            num = arena("num_c", O_X, [128, T], F32)
            den = arena("den_c", O_X + 8192, [128, T], F32)
            m_bf = arena("m_c", O_COMB, [128, NCT * T], BF16)
            xab = arena("xa_c", O_XBF, [128, NCT * TB], F32)

            tok_woa, _ = weight_load_cast(woaT, NCT, C, woa_sb, "woa")
            PP("vector", "memset", Afb[:, 0:1], 0.0)
            PP("vector", "memset", Bfb[:, 0:1], 0.0)
            PP("vector", "memset", Abb[:, T:T + 1], 0.0)
            tok_z = PP("vector", "memset", Bbb[:, T:T + 1], 0.0)

            tok_par = [tok_z, tok_z]
            tok_tk_prev = None
            tok_m_prev = None
            tok_m_last = None
            tok_wo_last = None
            tok_xm_st = None
            tok_rows_dma = None
            for s in range(SEQ):
                tok_m = [None] * NCT
                for ct in range(NCT):
                    par = ct % 2
                    gw = [tok_par[par]]
                    if s == 0 and ct < 2:
                        gw += [tok_woa[0], tok_woa[1]]
                    st = "s%d" % par
                    tok_lk = PP("sync", "dma_start", out=k2[par][:],
                                in_=k_d[s, ct * 128:(ct + 1) * 128, :],
                                waits=gw, dma=st)
                    tok_lv = PP("sync", "dma_start", out=v2[par][:],
                                in_=v_d[s, ct * 128:(ct + 1) * 128, :],
                                waits=gw, dma=st)
                    tok_lr = PP("sync", "dma_start", out=r2[par][:],
                                in_=r_d[s, ct * 128:(ct + 1) * 128, :],
                                waits=gw, dma=st)
                    ewcol = ew8[:, ct:ct + 1].broadcast_to([128, T])
                    eucol = eu8[:, ct:ct + 1]
                    t_m = PP("vector", "reduce_max", negM, k2[par][:], axis=AX,
                             negate=True, waits=[tok_lr, tok_tk_prev])
                    t_k = PP("scalar", "activation", kap, k2[par][:], AF.Exp,
                             bias=negM, waits=[t_m, tok_ew])
                    tok_tk_prev = t_k
                    t_d = PP("vector", "tensor_tensor", d1a, kap, v2[par][:],
                             OP.mult, waits=[t_k, tok_lr, tok_m_prev])
                    t_af = PP("vector", "tensor_tensor_scan", Afb[:, 1:T + 1],
                              ewcol, d1a, 0.0, OP.mult, OP.add, waits=[t_d])
                    t_bf = PP("vector", "tensor_tensor_scan", Bfb[:, 1:T + 1],
                              ewcol, kap, 0.0, OP.mult, OP.add, waits=[t_k])
                    t_n = PP("vector", "scalar_tensor_tensor", num, d1a, eucol,
                             Afb[:, 0:T], OP.mult, OP.add, waits=[t_af, tok_eu])
                    t_de = PP("vector", "scalar_tensor_tensor", den, kap, eucol,
                              Bfb[:, 0:T], OP.mult, OP.add, waits=[t_bf])
                    t_r1 = PP("vector", "reciprocal", den, den, waits=[t_de])
                    t_rf = PP("vector", "tensor_tensor", rfb, num, den, OP.mult,
                              waits=[t_r1, t_n])
                    t_ab = PP("vector", "tensor_tensor_scan", Abb[:, 0:T][:, ::-1],
                              ewcol, d1a[:, ::-1], 0.0, OP.mult, OP.add,
                              waits=[t_d])
                    t_bb = PP("vector", "tensor_tensor_scan", Bbb[:, 0:T][:, ::-1],
                              ewcol, kap[:, ::-1], 0.0, OP.mult, OP.add,
                              waits=[t_k])
                    t_n2 = PP("vector", "scalar_tensor_tensor", num, d1a, eucol,
                              Abb[:, 1:T + 1], OP.mult, OP.add, waits=[t_ab, t_rf])
                    t_d2 = PP("vector", "scalar_tensor_tensor", den, kap, eucol,
                              Bbb[:, 1:T + 1], OP.mult, OP.add, waits=[t_bb, t_r1])
                    t_r2 = PP("vector", "reciprocal", den, den, waits=[t_d2])
                    t_rb = PP("vector", "tensor_tensor", num, num, den, OP.mult,
                              waits=[t_r2, t_n2])
                    t_su = PP("vector", "tensor_tensor", num, num, rfb, OP.add,
                              waits=[t_rb])
                    tok_m[ct] = PP("vector", "scalar_tensor_tensor",
                                   m_bf[:, ct * T:(ct + 1) * T], num, 0.5,
                                   r2[par][:], OP.mult, OP.mult,
                                   waits=[t_su, tok_lr, tok_wo_last])
                    tok_par[par] = tok_m[ct]
                    tok_m_prev = tok_m[ct]
                tok_m_last = tok_m[NCT - 1]
                for tb in range(4):
                    t0 = tb * TB
                    b = s * 4 + tb
                    tok_lxa = None
                    for ci in range(NCT):
                        tok_lxa = PP("sync", "dma_start",
                                     out=xab[:, ci * TB:(ci + 1) * TB],
                                     in_=xa_d[s, ci * 128:(ci + 1) * 128, t0:t0 + TB],
                                     waits=[tok_xm_st], dma="sxa")
                    tok_ev = [None, None]
                    for cj in range(NCT):
                        bank = 4 + (cj % 2)
                        mmW = [((ps[bank][:],
                                 woa_sb[:, ci * C + cj * 128:ci * C + cj * 128 + 128],
                                 m_bf[:, ci * T + t0:ci * T + t0 + TB]),
                                dict(start=(ci == 0), stop=(ci == NCT - 1),
                                     skip_group_check=True))
                               for ci in range(NCT)]
                        ww = [tok_m_last, tok_ev[cj % 2]]
                        if s == 0 and tb == 0 and cj == 0:
                            ww += [tok_woa[0], tok_woa[1]]
                        tok_W = prog.mms(mmW, waits=ww)
                        tok_wo_last = tok_W
                        tok_ev[cj % 2] = PP("vector", "tensor_tensor",
                                            xab[:, cj * TB:(cj + 1) * TB],
                                            ps[bank][:],
                                            xab[:, cj * TB:(cj + 1) * TB],
                                            OP.add, waits=[tok_W, tok_lxa])
                    trs = stats_rows(xab, xsqc, tok_ev[0],
                                     extra_w=[tok_ev[1], tok_rows_dma])
                    PP("scalar", "dma_start", out=rows_2_d[b, 0:1, :],
                       in_=R_RSD, waits=[trs[0]], dma="grow")
                    tok_rows_dma = PP("scalar", "dma_start",
                                      out=rows_2_d[b, 1:2, :],
                                      in_=R_MURS, waits=[trs[1]], dma="grow")
                    for ci in range(NCT):
                        tok_xm_st = PP("scalar", "dma_start",
                            out=xm_d[s, ci * 128:(ci + 1) * 128, t0:t0 + TB],
                            in_=xab[:, ci * TB:(ci + 1) * TB],
                            waits=[trs[3], trs[4]], dma="gxm")

        def phase_e1():
            wfk_sb = arena("wfk_e", O_W1, [128, 8 * FF], BF16)
            wfr_sb = arena("wfr_e", O_W2, [128, 8 * C], BF16)
            srb = arena("sr_e", O_W2 + 16384, [128, NCT * TB], BF16)
            x8 = arena("x8_e", O_X, [128, NCT * TB], F32)
            t32 = arena("t32_e", O_T32, [128, NCT * TB], F32)
            xnbf = arena("xn_e", O_XBF, [128, NCT * TB], BF16)
            kkh = arena("kk_e", O_COMB, [128, 16 * TB], BF16)

            brow = arena("brow_e", O_W2 + 24576, [1, FF], F32)
            _, tcp = weight_load_cast(wfkT, NCT, FF, wfk_sb, "wfk",
                                      fold=ln2w8, bvec=ln2b8, bk_nf=32,
                                      bias_dst=bfk32, bias_col0=0, brow=brow)
            _, tcp2 = weight_load_cast(wfrT, NCT, C, wfr_sb, "wfr",
                                       fold=ln2w8, bvec=ln2b8, bk_nf=8,
                                       guard=tcp, bias_dst=bfrh8, bias_col0=0,
                                       bias_scale=0.5, brow=brow)

            tok_kst = None
            tok_rl = [None, None]
            tok_srst = None
            tok_xn_prev = None
            tok_sq_prev = None
            tok_taf_prev = None
            for b in range(NB):
                s, t0 = b // 4, (b % 4) * TB
                tok_ld = None
                for ci in range(NCT):
                    tok_ld = PP("sync", "dma_start",
                                out=x8[:, ci * TB:(ci + 1) * TB],
                                in_=xm_d[s, ci * 128:(ci + 1) * 128, t0:t0 + TB],
                                waits=[tok_xn_prev, tcp2 if b == 0 else None],
                                dma="sx")
                tr2 = PP("sync", "dma_start", out=rowf[:, 0:2 * TB],
                         in_=rows_2_d[b, :, :], waits=[tok_xn_prev], dma="srow")
                tok_sq_prev = tok_sq_prev if b > 0 else None
                tok_b1 = bcast(rowf[:, 0:TB], 1, [tr2])
                tok_b2 = bcast(rowf[:, TB:2 * TB], 2, [tr2])
                tok_t1, tok_xn = xn_from_bcast(x8, t32, xnbf, OP.subtract,
                                               tok_b1, tok_b2, tok_ld)
                tok_xn_prev = tok_xn
                for half in range(2):
                    for j in range(16):
                        fi = half * 16 + j
                        bank = 4 + (j % 2)
                        mmU = [((ps[bank][:],
                                 wfk_sb[:, ci * FF + fi * 128:ci * FF + fi * 128 + 128],
                                 xnbf[:, ci * TB:(ci + 1) * TB]),
                                dict(start=(ci == 0), stop=(ci == NCT - 1),
                                     skip_group_check=True))
                               for ci in range(NCT)]
                        tok_U = prog.mms(mmU, waits=[tok_xn, tok_kst,
                                                     tok_rl[j % 2]])
                        tok_rl[j % 2] = PP("vector", "tensor_scalar", tmpc,
                                           ps[bank][:], bfk32[:, fi:fi + 1], 0.0,
                                           OP.add, OP.max,
                                           waits=[tok_U, tok_sq_prev])
                        t_sq = PP("scalar", "activation",
                                  kkh[:, j * TB:(j + 1) * TB], tmpc, AF.Square,
                                  waits=[tok_rl[j % 2]])
                        tok_sq_prev = t_sq
                        tok_kst = PP("scalar", "dma_start",
                            out=kk_d[s, fi * 128:(fi + 1) * 128, t0:t0 + TB],
                            in_=kkh[:, j * TB:(j + 1) * TB],
                            waits=[t_sq], dma="gkk")
                for cj in range(NCT):
                    bank = 6 + (cj % 2)
                    mmR = [((ps[bank][:],
                             wfr_sb[:, ci * C + cj * 128:ci * C + cj * 128 + 128],
                             xnbf[:, ci * TB:(ci + 1) * TB]),
                            dict(start=(ci == 0), stop=(ci == NCT - 1),
                                 skip_group_check=True))
                           for ci in range(NCT)]
                    tok_R = prog.mms(mmR, waits=[tok_xn, tok_srst])
                    tt = PP("scalar", "activation", tmpc, ps[bank][:], AF.Tanh,
                            scale=0.5, bias=bfrh8[:, cj:cj + 1],
                            waits=[tok_R, tok_taf_prev if cj > 0 else tok_sq_prev])
                    taf = PP("vector", "tensor_scalar",
                             srb[:, cj * TB:(cj + 1) * TB], tmpc, 0.5, 0.5,
                             OP.mult, OP.add, waits=[tt])
                    tok_taf_prev = taf
                    tok_srst = PP("scalar", "dma_start",
                        out=sr_d[s, cj * 128:(cj + 1) * 128, t0:t0 + TB],
                        in_=srb[:, cj * TB:(cj + 1) * TB], waits=[taf],
                        dma="scr")

        def phase_e2():
            wfv_sb = arena("wfv_e2", O_W1, [128, 32 * C], BF16)
            kk_sb = arena("kk_e2", O_COMB, [128, 32 * TB], BF16)
            sr_sb = arena("sr_e2", O_XBF, [128, NCT * TB], BF16)
            x8 = arena("x8_e2", O_X, [128, NCT * TB], F32)
            h1b = arena("h1_e2", O_W2, [128, NCT * TB], F32)
            xsq = arena("xsq_e2", O_XSQ, [128, NCT * TB], BF16)

            tok_wfv, _ = weight_load_cast(wfvT, NFC, C, wfv_sb, "wfv")
            tok_h2st = None
            tok_free = None
            tok_rows_dma = None
            for b in range(NB):
                s, t0 = b // 4, (b % 4) * TB
                tok_lk = None
                for fj in range(NFC):
                    tok_lk = PP("sync", "dma_start",
                                out=kk_sb[:, fj * TB:(fj + 1) * TB],
                                in_=kk_d[s, fj * 128:(fj + 1) * 128, t0:t0 + TB],
                                waits=[tok_free,
                                       (tok_wfv[0] if b == 0 and fj == 0 else None),
                                       (tok_wfv[1] if b == 0 and fj == 0 else None)],
                                dma="s0")
                tok_ls = None
                for ci in range(NCT):
                    tok_ls = PP("sync", "dma_start",
                                out=sr_sb[:, ci * TB:(ci + 1) * TB],
                                in_=sr_d[s, ci * 128:(ci + 1) * 128, t0:t0 + TB],
                                waits=[tok_free], dma="s1")
                tok_lx = None
                for ci in range(NCT):
                    tok_lx = PP("sync", "dma_start",
                                out=x8[:, ci * TB:(ci + 1) * TB],
                                in_=xm_d[s, ci * 128:(ci + 1) * 128, t0:t0 + TB],
                                waits=[tok_h2st, tok_free], dma="sx")
                tok_lh = None
                for ci in range(NCT):
                    tok_lh = PP("sync", "dma_start",
                                out=h1b[:, ci * TB:(ci + 1) * TB],
                                in_=h1_d[s, ci * 128:(ci + 1) * 128, t0:t0 + TB],
                                waits=[tok_free], dma="s2")
                tok_xh = PP("vector", "tensor_tensor", h1b, h1b, x8, OP.add,
                            waits=[tok_lh, tok_lx])
                tok_ev = [None, None]
                tok_ev_prev = None
                for cj in range(NCT):
                    bank = 4 + (cj % 2)
                    mmV = [((ps[bank][:],
                             wfv_sb[:, fj * C + cj * 128:fj * C + cj * 128 + 128],
                             kk_sb[:, fj * TB:(fj + 1) * TB]),
                            dict(start=(fj == 0), stop=(fj == NFC - 1),
                                 skip_group_check=True))
                           for fj in range(NFC)]
                    tok_V = prog.mms(mmV, waits=[tok_lk, tok_ev[cj % 2]])
                    t_m = PP("vector", "tensor_tensor", tmpc, ps[bank][:],
                             sr_sb[:, cj * TB:(cj + 1) * TB], OP.mult,
                             waits=[tok_V, tok_ls, tok_ev_prev])
                    tok_ev[cj % 2] = PP("vector", "tensor_tensor",
                                        x8[:, cj * TB:(cj + 1) * TB], tmpc,
                                        h1b[:, cj * TB:(cj + 1) * TB], OP.add,
                                        waits=[t_m, tok_xh])
                    tok_ev_prev = tok_ev[cj % 2]
                    tok_h2st = PP("scalar", "dma_start",
                        out=h2_d[s, cj * 128:(cj + 1) * 128, t0:t0 + TB],
                        in_=x8[:, cj * TB:(cj + 1) * TB],
                        waits=[tok_ev[cj % 2]], dma="go")
                trs = stats_rows(x8, xsq, tok_ev[0],
                                 extra_w=[tok_ev[1], tok_rows_dma])
                PP("scalar", "dma_start", out=rows_f_d[b, 0:1, :],
                   in_=R_RSD, waits=[trs[0]], dma="grow")
                tok_rows_dma = PP("scalar", "dma_start", out=rows_f_d[b, 1:2, :],
                                  in_=R_MURS, waits=[trs[1]], dma="grow")
                tok_free = trs[1]

        if "a" in phases:
            phase_ffn(xT, h1_d, True, rows_a_d, None, "a")
            prog.barrier()
        if "b" in phases:
            phase_b()
            prog.barrier()
        if "c" in phases:
            phase_c()
            prog.barrier()
        if "e1" in phases:
            phase_e1()
            prog.barrier()
        if "e2" in phases:
            phase_e2()
            prog.barrier()
        if "f" in phases:
            phase_ffn(h2_d, outT, False, None, rows_f_d, "f")
        else:
            tok_dmy = PP("vector", "memset", tmpc[0:1, 0:1], 0.0)
            PP("scalar", "dma_start", out=outT[0, 0:1, 0:1],
               in_=tmpc[0:1, 0:1], waits=[tok_dmy], dma="go")

        block = es.enter_context(nc.Block())
        prog.emit(nc, block, es)

    return nc


def _prep_inputs(inputs):
    f32 = np.float32
    d = {}
    for key, src in [("wiT", "L0_ffn_wi_w"), ("woT", "L0_ffn_wo_w"),
                     ("wrT", "L0_attn_Wr"), ("wkT", "L0_attn_Wk"),
                     ("wvT", "L0_attn_Wv"), ("woaT", "L0_attn_Wo"),
                     ("wfkT", "L0_attn_Wfk"), ("wfvT", "L0_attn_Wfv"),
                     ("wfrT", "L0_attn_Wfr")]:
        d[key] = np.ascontiguousarray(np.asarray(inputs[src], dtype=f32).T)
    d["AsT"] = np.ascontiguousarray(
        np.asarray(inputs["L0_ffn_lora_As"], dtype=f32).transpose(0, 2, 1).reshape(8, C * 16))
    d["BsT"] = np.ascontiguousarray(
        np.asarray(inputs["L0_ffn_lora_Bs"], dtype=f32).transpose(0, 2, 1).reshape(32, 16384))

    def v8(name):
        return np.ascontiguousarray(
            np.asarray(inputs[name], dtype=f32).reshape(8, 128).T)

    d["ln1w"] = v8("L0_attn_ln1_w")
    d["ln1b"] = v8("L0_attn_ln1_b")
    d["ln2w"] = v8("L0_attn_ln2_w")
    d["ln2b"] = v8("L0_attn_ln2_b")
    d["dec"] = v8("L0_attn_decay")
    d["uu"] = v8("L0_attn_u")
    d["tki"] = np.ascontiguousarray(
        np.asarray(inputs["L0_ffn_top_k_indices"], dtype=np.int32).reshape(2, 1))
    d["tkv"] = np.ascontiguousarray(
        np.asarray(inputs["L0_ffn_top_k_values"], dtype=f32).reshape(1, 2))
    d["onesf"] = np.ones((128, 1), f32)
    d["onesb"] = np.ones((128, 1), BF)
    d["onesr"] = np.ones((65, 128), f32)
    return d


def kernel(**inputs):
    x = np.asarray(inputs["L0_attn_x"], dtype=np.float32)
    shared = _prep_inputs(inputs)
    in_maps = []
    for c in range(NCORE):
        m = dict(shared)
        m["xT"] = np.ascontiguousarray(x[c * SEQ:(c + 1) * SEQ].transpose(0, 2, 1))
        in_maps.append(m)
    nc = build_program()
    res = run_bass_kernel_spmd(nc, in_maps, core_ids=list(range(NCORE)))
    outs = [np.asarray(r["outT"]).transpose(0, 2, 1) for r in res.results]
    return np.ascontiguousarray(np.concatenate(outs, axis=0)).astype(np.float32)


# version 06
# speedup vs baseline: 1.0872x; 1.0872x over previous
import numpy as np
import ml_dtypes
from contextlib import ExitStack

import concourse.bass as bass
import concourse.mybir as mybir
from concourse.bass_utils import run_bass_kernel_spmd
from concourse.bass_core_rust import IndirectOffsetOnAxis

F32 = mybir.dt.float32
BF16 = mybir.dt.bfloat16
I32 = mybir.dt.int32
AF = mybir.ActivationFunctionType
OP = mybir.AluOpType
AX = mybir.AxisListType.X
BF = ml_dtypes.bfloat16

NCORE = 8
SEQ = 4
T = 2048
C = 1024
FF = 4096
TB = 512
NB = SEQ * T // TB
NCT = C // 128
NFC = FF // 128
EPS = 1e-5
LIMIT_BLOCKS = None
START_BLOCK = 0
SERIAL_DEBUG = False
A_STEP = 99
A_SUB = 'full'

O_W1 = 0
O_W2 = 65536
O_X = 131072
O_COMB = 147456
O_T32 = 163840
O_XBF = 180224
O_XSQ = 188416
ARENA_END = 196608
PO = ARENA_END
RESERVE = 224896


class Prog:

    ENGS = ("tensor", "vector", "scalar", "gpsimd", "sync")

    def __init__(self):
        self.recs = {e: [] for e in self.ENGS}
        self.count = {e: 0 for e in self.ENGS}
        self.pending_barrier = {e: None for e in self.ENGS}

    def op(self, eng, method, *args, waits=(), dma=None, **kwargs):
        key = eng if dma is None else dma
        inc = 1 if dma is None else 16
        if key not in self.count:
            self.count[key] = 0
        ws = {}
        if SERIAL_DEBUG:
            for k2, v in self.count.items():
                if k2 != eng and v > 0:
                    ws[k2] = v
        if self.pending_barrier[eng] is not None:
            for k2, v in self.pending_barrier[eng].items():
                if k2 != eng and v > 0:
                    ws[k2] = max(ws.get(k2, 0), v)
            self.pending_barrier[eng] = None
        for w in waits:
            if w is None:
                continue
            k2, v = w
            ws[k2] = max(ws.get(k2, 0), v)
        self.count[key] += inc
        self.recs[eng].append((method, args, kwargs, ws, inc, key))
        return (key, self.count[key])

    def mms(self, mmlist, waits=()):
        return self.op("tensor", "__mmgroup__", mmlist, waits=waits)

    def barrier(self):
        snap = dict(self.count)
        for e in self.ENGS:
            self.pending_barrier[e] = snap

    def emit(self, nc, block, es):
        keys = list(self.count.keys())
        sems = {k: es.enter_context(nc.semaphore("sem_" + k)) for k in keys}
        for ename in self.ENGS:
            recs = self.recs[ename]
            cnt = self.count

            def body(eng, recs=recs, ename=ename):
                waited = {k: 0 for k in keys}
                for method, args, kwargs, ws, inc, key in recs:
                    for k2, v in ws.items():
                        if v > waited[k2]:
                            eng.wait_ge(sems[k2], v)
                            waited[k2] = v
                    if method == "__mmgroup__":
                        inst = None
                        for margs, mkw in args[0]:
                            inst = eng.matmul(*margs, **mkw)
                    else:
                        inst = getattr(eng, method)(*args, **kwargs)
                    inst.then_inc(sems[key], inc)
                if ename == "gpsimd":
                    for k in keys:
                        if cnt[k] > 0:
                            eng.wait_ge(sems[k], cnt[k])

            getattr(block, ename)(body)


def build_program(dbg=(), phases=("a", "b", "c", "e1", "e2", "f")):
    nc = bass.Bass("TRN2", target_bir_lowering=False,
                   dynamic_dma_scratch_size=4096)

    def dram_in(name, shape, dtype=F32):
        return nc.dram_tensor(name, shape, dtype, kind="ExternalInput").ap()

    def dram_out(name, shape, dtype=F32):
        return nc.dram_tensor(name, shape, dtype, kind="ExternalOutput").ap()

    def dram_int(name, shape, dtype=F32):
        kind = "ExternalOutput" if name in dbg else "Internal"
        return nc.dram_tensor(name, shape, dtype, kind=kind).ap()

    xT = dram_in("xT", [SEQ, C, T])
    wiT = dram_in("wiT", [C, FF])
    woT = dram_in("woT", [FF, C])
    wrT = dram_in("wrT", [C, C])
    wkT = dram_in("wkT", [C, C])
    wvT = dram_in("wvT", [C, C])
    woaT = dram_in("woaT", [C, C])
    wfkT = dram_in("wfkT", [C, FF])
    wfvT = dram_in("wfvT", [FF, C])
    wfrT = dram_in("wfrT", [C, C])
    AsT = dram_in("AsT", [8, C * 16])
    BsT = dram_in("BsT", [32, 16384])
    ln1w = dram_in("ln1w", [128, 8])
    ln1b = dram_in("ln1b", [128, 8])
    ln2w = dram_in("ln2w", [128, 8])
    ln2b = dram_in("ln2b", [128, 8])
    dec = dram_in("dec", [128, 8])
    uu = dram_in("uu", [128, 8])
    tki = dram_in("tki", [2, 1], I32)
    tkv = dram_in("tkv", [1, 2])
    onesf_d = dram_in("onesf", [128, 1])
    onesb_d = dram_in("onesb", [128, 1], BF16)
    onesr_d = dram_in("onesr", [65, 128])

    outT = dram_out("outT", [SEQ, C, T])

    h1_d = dram_int("h1", [SEQ, C, T])
    xa_d = dram_int("xa", [SEQ, C, T])
    k_d = dram_int("kd", [SEQ, C, T])
    v_d = dram_int("vd", [SEQ, C, T])
    r_d = dram_int("rd", [SEQ, C, T], BF16)
    xm_d = dram_int("xm", [SEQ, C, T])
    kk_d = dram_int("kk", [SEQ, FF, T], BF16)
    sr_d = dram_int("sr", [SEQ, C, T], BF16)
    h2_d = dram_int("h2", [SEQ, C, T])
    rows_a_d = dram_int("rows_a", [NB, 3, TB])
    rows_2_d = dram_int("rows_2", [NB, 2, TB])
    rows_f_d = dram_int("rows_f", [NB, 2, TB])

    es = ExitStack()
    with es:
        BASE, _end = nc.bump_sbuf(RESERVE)

        def at(name, off, shape, dtype):
            return nc.alloc_sbuf_tensor_at(name, shape, dtype, offset=BASE + off)[:]

        pcur = [PO]

        def pers(name, shape, dtype):
            t = at(name, pcur[0], shape, dtype)
            sz = int(np.prod(shape[1:])) * mybir.dt.size(dtype)
            pcur[0] += (sz + 31) // 32 * 32
            return t

        tmpc = pers("tmpc", [128, TB], F32)
        tmpc_bf = pers("tmpc_bf", [128, TB], BF16)
        rowf = pers("rowf", [1, 6 * TB], F32)
        R_MU = rowf[:, 0:TB]
        R_VAR = rowf[:, TB:2 * TB]
        R_RSD = rowf[:, 2 * TB:3 * TB]
        R_MURS = rowf[:, 3 * TB:4 * TB]
        R_RS2 = rowf[:, 4 * TB:5 * TB]
        R_TMP = rowf[:, 5 * TB:6 * TB]
        ln1w8 = pers("ln1w8", [128, 8], F32)
        ln1b8 = pers("ln1b8", [128, 8], F32)
        ln2w8 = pers("ln2w8", [128, 8], F32)
        ln2b8 = pers("ln2b8", [128, 8], F32)
        dec8 = pers("dec8", [128, 8], F32)
        u8 = pers("u8", [128, 8], F32)
        ew8 = pers("ew8", [128, 8], F32)
        eu8 = pers("eu8", [128, 8], F32)
        cval = pers("cval", [128, 2], F32)
        onesf = pers("onesf_s", [128, 1], F32)
        onesb = pers("onesb_s", [128, 1], BF16)
        onesr = pers("onesr_s", [65, 128], F32)
        vt = pers("vt_s", [1, 2], F32)
        idx_sb = pers("idx_s", [2, 1], I32)
        idx4 = pers("idx4_s", [2, 1], I32)
        bkv = pers("bkv", [128, 24], F32)
        brh8 = pers("brh8", [128, 8], F32)
        bfk32 = pers("bfk32", [128, 32], F32)
        bfrh8 = pers("bfrh8", [128, 8], F32)
        A_sb = pers("A_sb", [128, 256], BF16)
        midcat = pers("midcat", [32, TB], BF16)
        mid0t = pers("mid0t", [16, TB], BF16)
        Bcat = pers("Bcat", [32, FF], BF16)
        Bsel0 = at("Bsel0", O_XSQ, [16, FF], BF16)
        assert pcur[0] <= RESERVE, f"persistent overflow {pcur[0]}"

        ps = [es.enter_context(nc.psum_tensor(f"ps{i}", [128, TB], F32))[:]
              for i in range(8)]

        prog = Prog()
        PP = prog.op

        def arena(name, off, shape, dtype):
            return at(name, off, shape, dtype)

        t_ld = {}
        for name, dst, src in [
            ("ln1w", ln1w8, ln1w), ("ln1b", ln1b8, ln1b),
            ("ln2w", ln2w8, ln2w), ("ln2b", ln2b8, ln2b),
            ("dec", dec8, dec), ("u", u8, uu),
            ("onesf", onesf, onesf_d), ("onesb", onesb, onesb_d),
            ("onesr", onesr, onesr_d), ("vt", vt, tkv), ("idx", idx_sb, tki),
        ]:
            t_ld[name] = PP("sync", "dma_start", out=dst, in_=src, dma="sv")
        tok_vecs = t_ld["idx"]

        tok_ed = PP("scalar", "activation", ew8, dec8, AF.Exp, waits=[tok_vecs])
        tok_ew = PP("scalar", "activation", ew8, ew8, AF.Exp, scale=-1.0, waits=[tok_ed])
        tok_eu = PP("scalar", "activation", eu8, u8, AF.Exp, waits=[tok_vecs])

        tok_cv1 = prog.mms([((ps[0][:, 0:2], onesr[0:1, :], vt),
                            dict(start=True, stop=True))],
                           waits=[tok_vecs])
        tok_cval = PP("vector", "tensor_copy", cval, ps[0][:, 0:2], waits=[tok_cv1])

        gtmp = arena("gtmp", O_W1, [2, 16384], F32)
        AstF = arena("AstF", O_T32, [128, 256], F32)
        BstF = arena("BstF", O_X, [48, FF], F32)

        tok_gA = PP("gpsimd", "indirect_dma_start", gtmp[:], None, AsT[:, :],
                    IndirectOffsetOnAxis(idx_sb, 0), waits=[tok_vecs], dma="g0")
        tA = None
        for ei in range(2):
            for ci in range(NCT):
                tA = PP("gpsimd", "dma_start",
                        out=AstF[:, (ei * 8 + ci) * 16:(ei * 8 + ci) * 16 + 16],
                        in_=gtmp[ei:ei + 1, ci * 2048:(ci + 1) * 2048],
                        waits=[tok_gA], dma="g0")
        tok_Acast = PP("vector", "tensor_copy", A_sb, AstF, waits=[tA])

        tok_moves = [tA]
        tok_i4 = None
        for ch in range(4):
            tok_i4 = PP("vector", "tensor_scalar", idx4, idx_sb, 4, ch,
                        OP.mult, OP.add, waits=[tok_vecs, tok_i4, tok_moves[-1]])
            tok_g = PP("gpsimd", "indirect_dma_start", gtmp[:], None,
                       BsT[:, :], IndirectOffsetOnAxis(idx4, 0),
                       waits=[tok_i4] + tok_moves[-2:], dma="g0")
            tok_moves.append(PP("gpsimd", "dma_start",
                                out=BstF[4 * ch:4 * ch + 4, :],
                                in_=gtmp[0:1, :], waits=[tok_g], dma="g0"))
            tok_moves.append(PP("gpsimd", "dma_start",
                                out=BstF[32 + 4 * ch:32 + 4 * ch + 4, :],
                                in_=gtmp[1:2, :], waits=[tok_g], dma="g0"))
        negtmp = arena("negtmp", O_COMB, [16, FF], BF16)
        tokB0 = PP("vector", "tensor_copy", Bsel0[0:16, :], BstF[0:16, :],
                   waits=tok_moves[1:])
        tokB1 = PP("vector", "tensor_copy", Bcat[0:16, :], BstF[32:48, :],
                   waits=tok_moves[1:] + [tokB0])
        tokBn = PP("vector", "tensor_scalar_mul", negtmp[0:16, :], BstF[0:16, :],
                   -1.0, waits=[tokB1])
        tok_Bdone = PP("gpsimd", "dma_start", out=Bcat[16:32, :],
                       in_=negtmp[0:16, :], waits=[tokBn], dma="g0")

        prog.barrier()

        def weight_load_cast(wdram, nchunk, chunk_cols, dst, tag, fold=None,
                             bvec=None, bk_nf=0, guard=None, bias_dst=None,
                             bias_col0=0, bias_scale=None, brow=None):
            stg = [arena(f"stg_{tag}_0", O_COMB, [128, chunk_cols], F32),
                   arena(f"stg_{tag}_1", O_T32, [128, chunk_cols], F32)]
            toks_cast = [guard, guard]
            toks_bk = [None, None]
            nsl = chunk_cols // TB
            for ci in range(nchunk):
                par = ci % 2
                tok_l = PP("sync", "dma_start", out=stg[par][:],
                           in_=wdram[ci * 128:(ci + 1) * 128, :],
                           waits=[toks_cast[par], toks_bk[par]],
                           dma="w%d" % par)
                if fold is not None:
                    toks_cast[par] = PP(
                        "vector", "tensor_scalar_mul",
                        dst[:, ci * chunk_cols:(ci + 1) * chunk_cols], stg[par][:],
                        fold[:, ci:ci + 1], waits=[tok_l])
                else:
                    toks_cast[par] = PP(
                        "vector", "tensor_copy",
                        dst[:, ci * chunk_cols:(ci + 1) * chunk_cols], stg[par][:],
                        waits=[tok_l])
                if bvec is not None:
                    mm = [((ps[sl][0:1, :], bvec[:, ci:ci + 1],
                            stg[par][:, sl * TB:(sl + 1) * TB]),
                           dict(start=(ci == 0), stop=(ci == nchunk - 1),
                                skip_group_check=True))
                          for sl in range(nsl)]
                    toks_bk[par] = prog.mms(mm, waits=[tok_l])
            tok_bias = None
            if bvec is not None:
                tok_mm = toks_bk[(nchunk - 1) % 2]
                t_ev = None
                for sl in range(nsl):
                    t_ev = PP("vector", "tensor_copy",
                              brow[0:1, sl * TB:(sl + 1) * TB], ps[sl][0:1, :],
                              waits=[tok_mm, t_ev])
                for fj in range(bk_nf):
                    tok_bias = PP("scalar", "dma_start",
                                  out=bias_dst[:, bias_col0 + fj:bias_col0 + fj + 1],
                                  in_=brow[0:1, fj * 128:fj * 128 + 128],
                                  waits=[t_ev], dma="sca")
                if bias_scale is not None:
                    tok_bias = PP("vector", "tensor_scalar_mul",
                                  bias_dst[:, bias_col0:bias_col0 + bk_nf],
                                  bias_dst[:, bias_col0:bias_col0 + bk_nf],
                                  bias_scale, waits=[tok_bias])
            return toks_cast, tok_bias

        def stats_rows(x8, xsq, tok_x, rs2=False, extra_w=()):
            w0 = [tok_x] + list(extra_w)
            tok_sq = PP("scalar", "activation", xsq, x8, AF.Square, waits=w0)
            mmA = [((ps[0][0:1, :], onesf, x8[:, ci * TB:(ci + 1) * TB]),
                    dict(start=(ci == 0), stop=(ci == NCT - 1),
                         skip_group_check=True)) for ci in range(NCT)]
            tok_s1 = prog.mms(mmA, waits=w0)
            mmB = [((ps[1][0:1, :], onesb, xsq[:, ci * TB:(ci + 1) * TB]),
                    dict(start=(ci == 0), stop=(ci == NCT - 1),
                         skip_group_check=True)) for ci in range(NCT)]
            tok_s2 = prog.mms(mmB, waits=[tok_sq])
            t1 = PP("vector", "tensor_scalar_mul", R_MU, ps[0][0:1, :], 1.0 / C,
                    waits=[tok_s1])
            t2 = PP("vector", "tensor_scalar_mul", R_VAR, ps[1][0:1, :], 1.0 / C,
                    waits=[tok_s2])
            t3 = PP("vector", "tensor_tensor", R_TMP, R_MU, R_MU, OP.mult,
                    waits=[t1])
            t4 = PP("vector", "tensor_tensor", R_VAR, R_VAR, R_TMP, OP.subtract,
                    waits=[t2, t3])
            t4b = PP("vector", "tensor_scalar_add", R_TMP, R_VAR, EPS, waits=[t4])
            t5 = PP("scalar", "activation", R_TMP, R_TMP, AF.Ln, waits=[t4b])
            t6 = PP("scalar", "activation", R_RSD, R_TMP, AF.Exp, scale=-0.5,
                    waits=[t5])
            t7 = PP("vector", "tensor_tensor", R_MURS, R_MU, R_RSD, OP.mult,
                    waits=[t6])
            out = [t6, t7, None, tok_sq, tok_s2]
            if rs2:
                u1 = PP("vector", "tensor_scalar_add", R_TMP, R_VAR, EPS,
                        waits=[t4, t5, t6])
                u2 = PP("vector", "reciprocal", R_RS2, R_TMP, waits=[u1])
                u3 = PP("vector", "tensor_tensor", R_RS2, R_VAR, R_RS2, OP.mult,
                        waits=[u2])
                u3b = PP("vector", "tensor_scalar_add", R_TMP, R_RS2, EPS,
                         waits=[u3])
                u4 = PP("scalar", "activation", R_TMP, R_TMP, AF.Ln,
                        waits=[u3b, t6])
                u5 = PP("scalar", "activation", R_RS2, R_TMP, AF.Exp, scale=-0.5,
                        waits=[u4])
                out[2] = u5
            return out

        def bcast(row_slice, bank, waits):
            b0 = row_slice.base_partition()
            return prog.mms([((ps[bank][:], onesr[b0:b0 + 1, :], row_slice),
                              dict(start=True, stop=True))], waits=waits)

        def xn_from_bcast(x8, t32, xnbf, op2, tok_b1, tok_b2, tok_x):
            x3 = x8.rearrange("p (a b) -> p a b", a=NCT)
            t3v = t32.rearrange("p (a b) -> p a b", a=NCT)
            xn3 = xnbf.rearrange("p (a b) -> p a b", a=NCT)
            bc1 = ps[1][:].unsqueeze(1).broadcast_to([128, NCT, TB])
            bc2 = ps[2][:].unsqueeze(1).broadcast_to([128, NCT, TB])
            ta = PP("vector", "tensor_tensor", t3v, x3, bc1, OP.mult,
                    waits=[tok_b1, tok_x])
            tb = PP("vector", "tensor_tensor", xn3, t3v, bc2, op2,
                    waits=[tok_b2, ta])
            return ta, tb

        def phase_ffn(src_d, dst_d, in_compute, rows_out_d, rows_in_d, tag):
            wi_sb = arena(f"wi_{tag}", O_W1, [128, 8 * FF], BF16)
            wo_sb = arena(f"wo_{tag}", O_W2, [128, 32 * C], BF16)
            x8 = arena(f"x8_{tag}", O_X, [128, NCT * TB], F32)
            comb = arena(f"comb_{tag}", O_COMB, [128, 16 * TB], BF16)
            t32 = arena(f"t32_{tag}", O_T32, [128, NCT * TB], F32)
            xnbf = arena(f"xn_{tag}", O_XBF, [128, NCT * TB], BF16)
            xsq = arena(f"xsq_{tag}", O_T32, [128, NCT * TB], BF16)

            if tag == "f":
                dF = PP("gpsimd", "dma_start", out=Bsel0[0:16, :],
                        in_=Bcat[16:32, :], dma="g0")
                tok_bs0 = PP("vector", "tensor_scalar_mul", Bsel0[0:16, :],
                             Bsel0[0:16, :], -1.0, waits=[dF])
            else:
                tok_bs0 = tok_Bdone
            tok_wi, _ = weight_load_cast(wiT, NCT, FF, wi_sb, "wi" + tag)
            tok_wo, _ = weight_load_cast(woT, NFC, C, wo_sb, "wo" + tag,
                                         guard=tok_wi[1])

            tok_store_last = None
            tok_xfree = None
            tok_rows_dma = None
            tok_midpk_prev = None
            tok_u2_last = None
            combq = [None, None, None]
            prev_add = None
            for b in range(START_BLOCK, LIMIT_BLOCKS or NB):
                s, t0 = b // 4, (b % 4) * TB
                tok_ld = None
                for ci in range(NCT):
                    w0b = [tok_store_last, tok_xfree]
                    if b == 0:
                        w0b += [tok_wi[0], tok_wi[1], tok_wo[0], tok_wo[1],
                                tok_Bdone, tok_bs0]
                    tok_ld = PP("sync", "dma_start",
                                out=x8[:, ci * TB:(ci + 1) * TB],
                                in_=src_d[s, ci * 128:(ci + 1) * 128, t0:t0 + TB],
                                waits=w0b, dma="sx")
                if A_STEP < 2:
                    tok_xfree = tok_ld
                    continue
                if in_compute:
                    trs = stats_rows(x8, xsq, tok_ld, extra_w=[tok_rows_dma])
                    tok_rsd, tok_murs = trs[0], trs[1]
                    row_rsd, row_murs = R_RSD, R_MURS
                    if "rows_2" in dbg:
                        PP("scalar", "dma_start", out=rows_2_d[b, 0:1, :],
                           in_=R_RSD, waits=[trs[0]], dma="grow")
                        PP("scalar", "dma_start", out=rows_2_d[b, 1:2, :],
                           in_=R_MURS, waits=[trs[1]], dma="grow")
                else:
                    tr2 = PP("sync", "dma_start", out=rowf[:, 0:2 * TB],
                             in_=rows_in_d[b, :, :], waits=[tok_xfree],
                             dma="srow")
                    tok_rsd = tok_murs = tr2
                    row_rsd, row_murs = rowf[:, 0:TB], rowf[:, TB:2 * TB]
                if A_STEP < 3:
                    tok_xfree = trs[-1]
                    continue
                tok_b1 = bcast(row_rsd, 1, [tok_rsd])
                tok_b2 = bcast(row_murs, 2, [tok_murs])
                tok_t1, tok_xn = xn_from_bcast(x8, t32, xnbf, OP.subtract,
                                               tok_b1, tok_b2, tok_ld)
                if "kk" in dbg:
                    for ci in range(NCT):
                        PP("scalar", "dma_start",
                           out=kk_d[s, (b % 4) * NCT * 128 + ci * 128:
                                    (b % 4) * NCT * 128 + ci * 128 + 128,
                                    t0:t0 + TB][0:128 - 0, :]
                           if False else kk_d[s, ci * 128:(ci + 1) * 128, t0:t0 + TB],
                           in_=xnbf[:, ci * TB:(ci + 1) * TB],
                           waits=[tok_xn], dma="grow")
                if A_STEP < 4:
                    tok_xfree = tok_xn
                    continue
                mmM = []
                for ei in range(2):
                    for ci in range(NCT):
                        mmM.append(((ps[3][ei * 32:ei * 32 + 16, :],
                                     A_sb[:, (ei * 8 + ci) * 16:(ei * 8 + ci) * 16 + 16],
                                     xnbf[:, ci * TB:(ci + 1) * TB]),
                                    dict(start=(ci == 0), stop=(ci == NCT - 1),
                                         skip_group_check=True)))
                tok_mid = prog.mms(mmM, waits=[tok_xn, tok_Acast])
                tok_midb0 = PP("vector", "tensor_copy", mid0t[0:16, :],
                               ps[3][0:16, :],
                               waits=[tok_mid, tok_midpk_prev])
                tok_midb1 = PP("vector", "tensor_copy", midcat[0:16, :],
                               ps[3][32:48, :], waits=[tok_mid, tok_midb0])
                tok_midpk = PP("scalar", "dma_start", out=midcat[16:32, :],
                               in_=mid0t[0:16, :],
                               waits=[tok_midb0, tok_u2_last], dma="gmid")
                tok_midbf = tok_midpk
                tok_midpk_prev = tok_midpk

                if A_STEP < 5:
                    tok_xfree = tok_midbf
                    continue
                tok_comb = [None] * 16
                tok_h1 = [None, None]
                pend = None
                last_add = [prev_add]

                def flush(pend_):
                    bank_, j_, fi_, tok_e0_ = pend_
                    tok_U2_ = prog.mms(
                        [((ps[bank_][:], Bcat[0:32, fi_ * 128:fi_ * 128 + 128],
                           midcat[0:32, :]),
                          dict(start=False, stop=True,
                               skip_group_check=True))], waits=[tok_e0_])
                    tok_e1_ = PP("vector", "tensor_scalar", tmpc_bf,
                                 ps[bank_][:], 0.0, cval[:, 1:2], OP.max,
                                 OP.mult, waits=[tok_U2_, last_add[0]])
                    tok_add_ = PP("vector", "tensor_tensor",
                                  comb[:, j_ * TB:(j_ + 1) * TB],
                                  comb[:, j_ * TB:(j_ + 1) * TB], tmpc_bf,
                                  OP.add, waits=[tok_e1_, tok_e0_])
                    combq[fi_ % 3] = tok_add_
                    last_add[0] = tok_add_
                    tok_comb[j_] = tok_add_
                    return tok_U2_

                for half in range(2):
                    for j in range(16):
                        fi = half * 16 + j
                        bank = 3 + (fi % 3)
                        mmU = [((ps[bank][:],
                                 wi_sb[:, ci * FF + fi * 128:ci * FF + fi * 128 + 128],
                                 xnbf[:, ci * TB:(ci + 1) * TB]),
                                dict(start=(ci == 0), stop=False,
                                     skip_group_check=True))
                               for ci in range(NCT)]
                        mmU.append(((ps[bank][:], Bsel0[0:16, fi * 128:fi * 128 + 128],
                                     mid0t[0:16, :]),
                                    dict(start=False, stop=True,
                                         skip_group_check=True)))
                        tok_U = prog.mms(mmU, waits=[tok_midbf, tok_midb1,
                                                     combq[fi % 3]])
                        tok_e0 = PP("scalar", "activation",
                                    comb[:, j * TB:(j + 1) * TB], ps[bank][:],
                                    AF.Relu, scale=cval[:, 0:1],
                                    waits=[tok_U, tok_cval])
                        if pend is not None:
                            tok_u2_last = flush(pend)
                        pend = (bank, j, fi, tok_e0)
                    tok_u2_last = flush(pend)
                    pend = None
                    for cj in range(NCT):
                        bank = 6 + (cj % 2)
                        mmD = [((ps[bank][:],
                                 wo_sb[:, (half * 16 + fj) * C + cj * 128:
                                       (half * 16 + fj) * C + cj * 128 + 128],
                                 comb[:, fj * TB:(fj + 1) * TB]),
                                dict(start=(fj == 0), stop=(fj == 15),
                                     skip_group_check=True))
                               for fj in range(16)]
                        tok_D = prog.mms(mmD, waits=[tok_comb[15],
                                                     tok_h1[cj % 2]])
                        tok_h1[cj % 2] = PP("vector", "scalar_tensor_tensor",
                                            x8[:, cj * TB:(cj + 1) * TB],
                                            ps[bank][:], 0.5,
                                            x8[:, cj * TB:(cj + 1) * TB],
                                            OP.mult, OP.add, waits=[tok_D, tok_t1])
                        if half == 1:
                            tok_store_last = PP("scalar", "dma_start",
                                out=dst_d[s, cj * 128:(cj + 1) * 128, t0:t0 + TB],
                                in_=x8[:, cj * TB:(cj + 1) * TB],
                                waits=[tok_h1[cj % 2]], dma="go")
                if A_STEP < 6:
                    tok_xfree = tok_comb[15]
                    continue
                if A_STEP < 7:
                    tok_xfree = tok_h1[1]
                    continue
                prev_add = last_add[0]
                if rows_out_d is not None:
                    trs = stats_rows(x8, xsq, tok_h1[0], rs2=True,
                                     extra_w=[tok_h1[1], tok_rows_dma])
                    d1 = PP("scalar", "dma_start", out=rows_out_d[b, 0:1, :],
                            in_=R_RSD, waits=[trs[0]], dma="grow")
                    d2 = PP("scalar", "dma_start", out=rows_out_d[b, 1:2, :],
                            in_=R_MURS, waits=[trs[1]], dma="grow")
                    d3 = PP("scalar", "dma_start", out=rows_out_d[b, 2:3, :],
                            in_=R_RS2, waits=[trs[2]], dma="grow")
                    tok_rows_dma = d3
                    tok_xfree = trs[4]
                else:
                    tok_xfree = tok_h1[1]

        def phase_b():
            wr_sb = arena("wr_b", O_W1, [128, 8 * C], BF16)
            wk_sb = arena("wk_b", O_W1 + 16384, [128, 8 * C], BF16)
            wv_sb = arena("wv_b", O_W1 + 32768, [128, 8 * C], BF16)
            kbuf = arena("kbuf_b", O_W2, [128, NCT * TB], F32)
            vbuf = arena("vbuf_b", O_W2 + 16384, [128, NCT * TB], F32)
            rbuf = arena("rbuf_b", O_W2 + 32768, [128, NCT * TB], BF16)
            x8 = arena("x8_b", O_X, [128, NCT * TB], F32)
            t32 = arena("t32_b", O_T32, [128, NCT * TB], F32)
            xnbf = arena("xn_b", O_XBF, [128, NCT * TB], BF16)

            brow = arena("brow_b", O_XBF, [1, C], F32)
            _, tok_bkr = weight_load_cast(wrT, NCT, C, wr_sb, "wr", fold=ln1w8,
                                          bvec=ln1b8, bk_nf=8,
                                          bias_dst=bkv, bias_col0=0, brow=brow)
            teh = PP("vector", "tensor_scalar_mul", brh8, bkv[:, 0:8], 0.5,
                     waits=[tok_bkr])
            _, tok_bkk = weight_load_cast(wkT, NCT, C, wk_sb, "wk", fold=ln1w8,
                                          bvec=ln1b8, bk_nf=8, guard=teh,
                                          bias_dst=bkv, bias_col0=8, brow=brow)
            _, te3 = weight_load_cast(wvT, NCT, C, wv_sb, "wv", fold=ln1w8,
                                      bvec=ln1b8, bk_nf=8, guard=tok_bkk,
                                      bias_dst=bkv, bias_col0=16, brow=brow)
            if "h2" in dbg:
                PP("scalar", "dma_start", out=h2_d[0, 0:128, 0:24],
                   in_=bkv[:, 0:24], waits=[te3], dma="grow")
                PP("scalar", "dma_start", out=h2_d[0, 0:128, 24:32],
                   in_=brh8[:, 0:8], waits=[te3], dma="grow")

            tok_st = {"r": None, "k": None, "v": None, "xa": None}
            tok_xn_prev = None
            for b in range(NB):
                s, t0 = b // 4, (b % 4) * TB
                tok_ld = None
                for ci in range(NCT):
                    tok_ld = PP("sync", "dma_start",
                                out=x8[:, ci * TB:(ci + 1) * TB],
                                in_=h1_d[s, ci * 128:(ci + 1) * 128, t0:t0 + TB],
                                waits=[tok_xn_prev,
                                       te3 if b == 0 else None], dma="sx")
                tr = PP("sync", "dma_start", out=rowf[:, 0:3 * TB],
                        in_=rows_a_d[b, :, :], waits=[tok_xn_prev], dma="srow")
                tok_b1 = bcast(rowf[:, 0:TB], 1, [tr])
                tok_b2 = bcast(rowf[:, TB:2 * TB], 2, [tr])
                tok_b3 = bcast(rowf[:, 2 * TB:3 * TB], 3, [tr, te3])
                x3 = x8.rearrange("p (a b) -> p a b", a=NCT)
                t3v = t32.rearrange("p (a b) -> p a b", a=NCT)
                xn3 = xnbf.rearrange("p (a b) -> p a b", a=NCT)
                bc1 = ps[1][:].unsqueeze(1).broadcast_to([128, NCT, TB])
                bc2 = ps[2][:].unsqueeze(1).broadcast_to([128, NCT, TB])
                bc3 = ps[3][:].unsqueeze(1).broadcast_to([128, NCT, TB])
                ta = PP("vector", "tensor_tensor", t3v, x3, bc1, OP.mult,
                        waits=[tok_b1, tok_ld, tok_st["xa"]])
                tok_xa = PP("vector", "tensor_tensor", t3v, t3v, bc2, OP.subtract,
                            waits=[tok_b2, ta])
                for ci in range(NCT):
                    tok_st["xa"] = PP("scalar", "dma_start",
                        out=xa_d[s, ci * 128:(ci + 1) * 128, t0:t0 + TB],
                        in_=t32[:, ci * TB:(ci + 1) * TB],
                        waits=[tok_xa], dma="sca")
                tok_xn = PP("vector", "tensor_tensor", xn3, t3v, bc3, OP.mult,
                            waits=[tok_b3, tok_xa])
                tok_xn_prev = tok_xn
                if "kk" in dbg:
                    for ci in range(NCT):
                        PP("scalar", "dma_start",
                           out=kk_d[s, ci * 128:(ci + 1) * 128, t0:t0 + TB],
                           in_=xnbf[:, ci * TB:(ci + 1) * TB],
                           waits=[tok_xn], dma="grow")
                tok_evprev = None
                pq = [None, None, None, None]
                pqi = 0
                for pi in range(3):
                    name = ["r", "k", "v"][pi]
                    wsb = [wr_sb, wk_sb, wv_sb][pi]
                    for cj in range(NCT):
                        bank = 4 + (pqi % 4)
                        mmP = [((ps[bank][:],
                                 wsb[:, ci * C + cj * 128:ci * C + cj * 128 + 128],
                                 xnbf[:, ci * TB:(ci + 1) * TB]),
                                dict(start=(ci == 0), stop=(ci == NCT - 1),
                                     skip_group_check=True))
                               for ci in range(NCT)]
                        tok_P = prog.mms(mmP, waits=[tok_xn, tok_st[name],
                                                     pq[pqi % 4]])
                        if pi == 0:
                            tt = PP("scalar", "activation", tmpc, ps[bank][:],
                                    AF.Tanh, scale=0.5, bias=brh8[:, cj:cj + 1],
                                    waits=[tok_P, tok_evprev])
                            tok_ev = PP("vector", "tensor_scalar",
                                        rbuf[:, cj * TB:(cj + 1) * TB], tmpc,
                                        0.5, 0.5, OP.mult, OP.add, waits=[tt])
                            tok_evprev = tok_ev
                            buf, dd, q = rbuf, r_d, "scalar"
                        else:
                            buf = kbuf if pi == 1 else vbuf
                            dd = k_d if pi == 1 else v_d
                            q = "scalar"
                            tok_ev = PP("vector", "tensor_scalar_add",
                                        buf[:, cj * TB:(cj + 1) * TB], ps[bank][:],
                                        bkv[:, 8 * pi + cj:8 * pi + cj + 1],
                                        waits=[tok_P])
                            tok_evprev = tok_ev
                        tok_st[name] = PP(q, "dma_start",
                            out=dd[s, cj * 128:(cj + 1) * 128, t0:t0 + TB],
                            in_=buf[:, cj * TB:(cj + 1) * TB],
                            waits=[tok_ev],
                            dma={"r": "scr", "k": "gk", "v": "gv"}[name])
                        pq[pqi % 4] = tok_ev
                        pqi += 1

        def phase_c():
            woa_sb = arena("woa_c", O_W1, [128, 8 * C], BF16)
            k2 = [arena("k2a", O_W1 + 16384, [128, T], F32),
                  arena("k2b", O_W1 + 24576, [128, T], F32)]
            v2 = [arena("v2a", O_W1 + 32768, [128, T], F32),
                  arena("v2b", O_W1 + 40960, [128, T], F32)]
            r2 = [arena("r2a", O_W1 + 49152, [128, T], BF16),
                  arena("r2b", O_W1 + 53248, [128, T], BF16)]
            xsqc = arena("xsq_c", O_W1 + 57344, [128, NCT * TB], BF16)
            kap = arena("kap_c", O_W2, [128, T], F32)
            d1a = arena("d1a_c", O_W2 + 8192, [128, T], F32)
            Afb = arena("Af_c", O_W2 + 16384, [128, T + 4], F32)
            Bfb = arena("Bf_c", O_W2 + 24832, [128, T + 4], F32)
            Abb = arena("Ab_c", O_W2 + 33280, [128, T + 4], F32)
            Bbb = arena("Bb_c", O_W2 + 41728, [128, T + 4], F32)
            rfb = arena("rf_c", O_W2 + 50176, [128, T], F32)
            negM = arena("negM_c", O_W2 + 58368, [128, 1], F32)
            num = arena("num_c", O_X, [128, T], F32)
            den = arena("den_c", O_X + 8192, [128, T], F32)
            m_bf = arena("m_c", O_COMB, [128, NCT * T], BF16)
            xab = arena("xa_c", O_XBF, [128, NCT * TB], F32)

            tok_woa, _ = weight_load_cast(woaT, NCT, C, woa_sb, "woa")
            PP("vector", "memset", Afb[:, 0:1], 0.0)
            PP("vector", "memset", Bfb[:, 0:1], 0.0)
            PP("vector", "memset", Abb[:, T:T + 1], 0.0)
            tok_z = PP("vector", "memset", Bbb[:, T:T + 1], 0.0)

            tok_par = [tok_z, tok_z]
            tok_tk_prev = None
            tok_m_prev = None
            tok_m_last = None
            tok_wo_last = None
            tok_xm_st = None
            tok_rows_dma = None
            for s in range(SEQ):
                tok_m = [None] * NCT
                for ct in range(NCT):
                    par = ct % 2
                    gw = [tok_par[par]]
                    if s == 0 and ct < 2:
                        gw += [tok_woa[0], tok_woa[1]]
                    st = "s%d" % par
                    tok_lk = PP("sync", "dma_start", out=k2[par][:],
                                in_=k_d[s, ct * 128:(ct + 1) * 128, :],
                                waits=gw, dma=st)
                    tok_lv = PP("sync", "dma_start", out=v2[par][:],
                                in_=v_d[s, ct * 128:(ct + 1) * 128, :],
                                waits=gw, dma=st)
                    tok_lr = PP("sync", "dma_start", out=r2[par][:],
                                in_=r_d[s, ct * 128:(ct + 1) * 128, :],
                                waits=gw, dma=st)
                    ewcol = ew8[:, ct:ct + 1].broadcast_to([128, T])
                    eucol = eu8[:, ct:ct + 1]
                    t_m = PP("vector", "reduce_max", negM, k2[par][:], axis=AX,
                             negate=True, waits=[tok_lr, tok_tk_prev])
                    t_k = PP("scalar", "activation", kap, k2[par][:], AF.Exp,
                             bias=negM, waits=[t_m, tok_ew])
                    tok_tk_prev = t_k
                    t_d = PP("vector", "tensor_tensor", d1a, kap, v2[par][:],
                             OP.mult, waits=[t_k, tok_lr, tok_m_prev])
                    t_af = PP("vector", "tensor_tensor_scan", Afb[:, 1:T + 1],
                              ewcol, d1a, 0.0, OP.mult, OP.add, waits=[t_d])
                    t_bf = PP("vector", "tensor_tensor_scan", Bfb[:, 1:T + 1],
                              ewcol, kap, 0.0, OP.mult, OP.add, waits=[t_k])
                    t_n = PP("vector", "scalar_tensor_tensor", num, d1a, eucol,
                             Afb[:, 0:T], OP.mult, OP.add, waits=[t_af, tok_eu])
                    t_de = PP("vector", "scalar_tensor_tensor", den, kap, eucol,
                              Bfb[:, 0:T], OP.mult, OP.add, waits=[t_bf])
                    t_r1 = PP("vector", "reciprocal", den, den, waits=[t_de])
                    t_rf = PP("vector", "tensor_tensor", rfb, num, den, OP.mult,
                              waits=[t_r1, t_n])
                    t_ab = PP("vector", "tensor_tensor_scan", Abb[:, 0:T][:, ::-1],
                              ewcol, d1a[:, ::-1], 0.0, OP.mult, OP.add,
                              waits=[t_d])
                    t_bb = PP("vector", "tensor_tensor_scan", Bbb[:, 0:T][:, ::-1],
                              ewcol, kap[:, ::-1], 0.0, OP.mult, OP.add,
                              waits=[t_k])
                    t_n2 = PP("vector", "scalar_tensor_tensor", num, d1a, eucol,
                              Abb[:, 1:T + 1], OP.mult, OP.add, waits=[t_ab, t_rf])
                    t_d2 = PP("vector", "scalar_tensor_tensor", den, kap, eucol,
                              Bbb[:, 1:T + 1], OP.mult, OP.add, waits=[t_bb, t_r1])
                    t_r2 = PP("vector", "reciprocal", den, den, waits=[t_d2])
                    t_rb = PP("vector", "tensor_tensor", num, num, den, OP.mult,
                              waits=[t_r2, t_n2])
                    t_su = PP("vector", "tensor_tensor", num, num, rfb, OP.add,
                              waits=[t_rb])
                    tok_m[ct] = PP("vector", "scalar_tensor_tensor",
                                   m_bf[:, ct * T:(ct + 1) * T], num, 0.5,
                                   r2[par][:], OP.mult, OP.mult,
                                   waits=[t_su, tok_lr, tok_wo_last])
                    tok_par[par] = tok_m[ct]
                    tok_m_prev = tok_m[ct]
                tok_m_last = tok_m[NCT - 1]
                for tb in range(4):
                    t0 = tb * TB
                    b = s * 4 + tb
                    tok_lxa = None
                    for ci in range(NCT):
                        tok_lxa = PP("sync", "dma_start",
                                     out=xab[:, ci * TB:(ci + 1) * TB],
                                     in_=xa_d[s, ci * 128:(ci + 1) * 128, t0:t0 + TB],
                                     waits=[tok_xm_st], dma="sxa")
                    tok_ev = [None, None, None]
                    for cj in range(NCT):
                        bank = 3 + (cj % 3)
                        mmW = [((ps[bank][:],
                                 woa_sb[:, ci * C + cj * 128:ci * C + cj * 128 + 128],
                                 m_bf[:, ci * T + t0:ci * T + t0 + TB]),
                                dict(start=(ci == 0), stop=(ci == NCT - 1),
                                     skip_group_check=True))
                               for ci in range(NCT)]
                        ww = [tok_m_last, tok_ev[cj % 3]]
                        if s == 0 and tb == 0 and cj == 0:
                            ww += [tok_woa[0], tok_woa[1]]
                        tok_W = prog.mms(mmW, waits=ww)
                        tok_wo_last = tok_W
                        tok_ev[cj % 3] = PP("vector", "tensor_tensor",
                                            xab[:, cj * TB:(cj + 1) * TB],
                                            ps[bank][:],
                                            xab[:, cj * TB:(cj + 1) * TB],
                                            OP.add, waits=[tok_W, tok_lxa])
                    trs = stats_rows(xab, xsqc, tok_ev[1],
                                     extra_w=[tok_ev[0], tok_ev[2],
                                              tok_rows_dma])
                    PP("scalar", "dma_start", out=rows_2_d[b, 0:1, :],
                       in_=R_RSD, waits=[trs[0]], dma="grow")
                    tok_rows_dma = PP("scalar", "dma_start",
                                      out=rows_2_d[b, 1:2, :],
                                      in_=R_MURS, waits=[trs[1]], dma="grow")
                    for ci in range(NCT):
                        tok_xm_st = PP("scalar", "dma_start",
                            out=xm_d[s, ci * 128:(ci + 1) * 128, t0:t0 + TB],
                            in_=xab[:, ci * TB:(ci + 1) * TB],
                            waits=[trs[3], trs[4]], dma="gxm")

        def phase_e1():
            wfk_sb = arena("wfk_e", O_W1, [128, 8 * FF], BF16)
            wfr_sb = arena("wfr_e", O_W2, [128, 8 * C], BF16)
            srb = arena("sr_e", O_W2 + 16384, [128, NCT * TB], BF16)
            x8 = arena("x8_e", O_X, [128, NCT * TB], F32)
            t32 = arena("t32_e", O_T32, [128, NCT * TB], F32)
            xnbf = arena("xn_e", O_XBF, [128, NCT * TB], BF16)
            kkh = arena("kk_e", O_COMB, [128, 16 * TB], BF16)

            brow = arena("brow_e", O_W2 + 24576, [1, FF], F32)
            _, tcp = weight_load_cast(wfkT, NCT, FF, wfk_sb, "wfk",
                                      fold=ln2w8, bvec=ln2b8, bk_nf=32,
                                      bias_dst=bfk32, bias_col0=0, brow=brow)
            _, tcp2 = weight_load_cast(wfrT, NCT, C, wfr_sb, "wfr",
                                       fold=ln2w8, bvec=ln2b8, bk_nf=8,
                                       guard=tcp, bias_dst=bfrh8, bias_col0=0,
                                       bias_scale=0.5, brow=brow)

            tok_kst = None
            tok_rl = [None, None, None]
            tok_srst = None
            tok_xn_prev = None
            tok_sq_prev = None
            tok_taf_prev = None
            for b in range(NB):
                s, t0 = b // 4, (b % 4) * TB
                tok_ld = None
                for ci in range(NCT):
                    tok_ld = PP("sync", "dma_start",
                                out=x8[:, ci * TB:(ci + 1) * TB],
                                in_=xm_d[s, ci * 128:(ci + 1) * 128, t0:t0 + TB],
                                waits=[tok_xn_prev, tcp2 if b == 0 else None],
                                dma="sx")
                tr2 = PP("sync", "dma_start", out=rowf[:, 0:2 * TB],
                         in_=rows_2_d[b, :, :], waits=[tok_xn_prev], dma="srow")
                tok_sq_prev = tok_sq_prev if b > 0 else None
                tok_b1 = bcast(rowf[:, 0:TB], 1, [tr2])
                tok_b2 = bcast(rowf[:, TB:2 * TB], 2, [tr2])
                tok_t1, tok_xn = xn_from_bcast(x8, t32, xnbf, OP.subtract,
                                               tok_b1, tok_b2, tok_ld)
                tok_xn_prev = tok_xn
                for half in range(2):
                    for j in range(16):
                        fi = half * 16 + j
                        bank = 3 + (fi % 3)
                        mmU = [((ps[bank][:],
                                 wfk_sb[:, ci * FF + fi * 128:ci * FF + fi * 128 + 128],
                                 xnbf[:, ci * TB:(ci + 1) * TB]),
                                dict(start=(ci == 0), stop=(ci == NCT - 1),
                                     skip_group_check=True))
                               for ci in range(NCT)]
                        tok_U = prog.mms(mmU, waits=[tok_xn, tok_kst,
                                                     tok_rl[fi % 3]])
                        tok_rl[fi % 3] = PP("vector", "tensor_scalar", tmpc,
                                            ps[bank][:], bfk32[:, fi:fi + 1], 0.0,
                                            OP.add, OP.max,
                                            waits=[tok_U, tok_sq_prev])
                        t_sq = PP("scalar", "activation",
                                  kkh[:, j * TB:(j + 1) * TB], tmpc, AF.Square,
                                  waits=[tok_rl[fi % 3]])
                        tok_sq_prev = t_sq
                        tok_kst = PP("scalar", "dma_start",
                            out=kk_d[s, fi * 128:(fi + 1) * 128, t0:t0 + TB],
                            in_=kkh[:, j * TB:(j + 1) * TB],
                            waits=[t_sq], dma="gkk")
                for cj in range(NCT):
                    bank = 6 + (cj % 2)
                    mmR = [((ps[bank][:],
                             wfr_sb[:, ci * C + cj * 128:ci * C + cj * 128 + 128],
                             xnbf[:, ci * TB:(ci + 1) * TB]),
                            dict(start=(ci == 0), stop=(ci == NCT - 1),
                                 skip_group_check=True))
                           for ci in range(NCT)]
                    tok_R = prog.mms(mmR, waits=[tok_xn, tok_srst])
                    tt = PP("scalar", "activation", tmpc, ps[bank][:], AF.Tanh,
                            scale=0.5, bias=bfrh8[:, cj:cj + 1],
                            waits=[tok_R, tok_taf_prev if cj > 0 else tok_sq_prev])
                    taf = PP("vector", "tensor_scalar",
                             srb[:, cj * TB:(cj + 1) * TB], tmpc, 0.5, 0.5,
                             OP.mult, OP.add, waits=[tt])
                    tok_taf_prev = taf
                    tok_srst = PP("scalar", "dma_start",
                        out=sr_d[s, cj * 128:(cj + 1) * 128, t0:t0 + TB],
                        in_=srb[:, cj * TB:(cj + 1) * TB], waits=[taf],
                        dma="scr")

        def phase_e2():
            wfv_sb = arena("wfv_e2", O_W1, [128, 32 * C], BF16)
            kk_sb = arena("kk_e2", O_COMB, [128, 32 * TB], BF16)
            sr_sb = arena("sr_e2", O_XBF, [128, NCT * TB], BF16)
            x8 = arena("x8_e2", O_X, [128, NCT * TB], F32)
            h1b = arena("h1_e2", O_W2, [128, NCT * TB], F32)
            xsq = arena("xsq_e2", O_XSQ, [128, NCT * TB], BF16)

            tok_wfv, _ = weight_load_cast(wfvT, NFC, C, wfv_sb, "wfv")
            tok_h2st = None
            tok_free = None
            tok_rows_dma = None
            for b in range(NB):
                s, t0 = b // 4, (b % 4) * TB
                tok_lk = None
                for fj in range(NFC):
                    tok_lk = PP("sync", "dma_start",
                                out=kk_sb[:, fj * TB:(fj + 1) * TB],
                                in_=kk_d[s, fj * 128:(fj + 1) * 128, t0:t0 + TB],
                                waits=[tok_free,
                                       (tok_wfv[0] if b == 0 and fj == 0 else None),
                                       (tok_wfv[1] if b == 0 and fj == 0 else None)],
                                dma="s0")
                tok_ls = None
                for ci in range(NCT):
                    tok_ls = PP("sync", "dma_start",
                                out=sr_sb[:, ci * TB:(ci + 1) * TB],
                                in_=sr_d[s, ci * 128:(ci + 1) * 128, t0:t0 + TB],
                                waits=[tok_free], dma="s1")
                tok_lx = None
                for ci in range(NCT):
                    tok_lx = PP("sync", "dma_start",
                                out=x8[:, ci * TB:(ci + 1) * TB],
                                in_=xm_d[s, ci * 128:(ci + 1) * 128, t0:t0 + TB],
                                waits=[tok_h2st, tok_free], dma="sx")
                tok_lh = None
                for ci in range(NCT):
                    tok_lh = PP("sync", "dma_start",
                                out=h1b[:, ci * TB:(ci + 1) * TB],
                                in_=h1_d[s, ci * 128:(ci + 1) * 128, t0:t0 + TB],
                                waits=[tok_free], dma="s2")
                tok_xh = PP("vector", "tensor_tensor", h1b, h1b, x8, OP.add,
                            waits=[tok_lh, tok_lx])
                tok_ev = [None, None]
                tok_ev_prev = None
                for cj in range(NCT):
                    bank = 4 + (cj % 2)
                    mmV = [((ps[bank][:],
                             wfv_sb[:, fj * C + cj * 128:fj * C + cj * 128 + 128],
                             kk_sb[:, fj * TB:(fj + 1) * TB]),
                            dict(start=(fj == 0), stop=(fj == NFC - 1),
                                 skip_group_check=True))
                           for fj in range(NFC)]
                    tok_V = prog.mms(mmV, waits=[tok_lk, tok_ev[cj % 2]])
                    t_m = PP("vector", "tensor_tensor", tmpc, ps[bank][:],
                             sr_sb[:, cj * TB:(cj + 1) * TB], OP.mult,
                             waits=[tok_V, tok_ls, tok_ev_prev])
                    tok_ev[cj % 2] = PP("vector", "tensor_tensor",
                                        x8[:, cj * TB:(cj + 1) * TB], tmpc,
                                        h1b[:, cj * TB:(cj + 1) * TB], OP.add,
                                        waits=[t_m, tok_xh])
                    tok_ev_prev = tok_ev[cj % 2]
                    tok_h2st = PP("scalar", "dma_start",
                        out=h2_d[s, cj * 128:(cj + 1) * 128, t0:t0 + TB],
                        in_=x8[:, cj * TB:(cj + 1) * TB],
                        waits=[tok_ev[cj % 2]], dma="go")
                trs = stats_rows(x8, xsq, tok_ev[0],
                                 extra_w=[tok_ev[1], tok_rows_dma])
                PP("scalar", "dma_start", out=rows_f_d[b, 0:1, :],
                   in_=R_RSD, waits=[trs[0]], dma="grow")
                tok_rows_dma = PP("scalar", "dma_start", out=rows_f_d[b, 1:2, :],
                                  in_=R_MURS, waits=[trs[1]], dma="grow")
                tok_free = trs[1]

        if "a" in phases:
            phase_ffn(xT, h1_d, True, rows_a_d, None, "a")
            prog.barrier()
        if "b" in phases:
            phase_b()
            prog.barrier()
        if "c" in phases:
            phase_c()
            prog.barrier()
        if "e1" in phases:
            phase_e1()
            prog.barrier()
        if "e2" in phases:
            phase_e2()
            prog.barrier()
        if "f" in phases:
            phase_ffn(h2_d, outT, False, None, rows_f_d, "f")
        else:
            tok_dmy = PP("vector", "memset", tmpc[0:1, 0:1], 0.0)
            PP("scalar", "dma_start", out=outT[0, 0:1, 0:1],
               in_=tmpc[0:1, 0:1], waits=[tok_dmy], dma="go")

        block = es.enter_context(nc.Block())
        prog.emit(nc, block, es)

    return nc


def _prep_inputs(inputs):
    f32 = np.float32
    d = {}
    for key, src in [("wiT", "L0_ffn_wi_w"), ("woT", "L0_ffn_wo_w"),
                     ("wrT", "L0_attn_Wr"), ("wkT", "L0_attn_Wk"),
                     ("wvT", "L0_attn_Wv"), ("woaT", "L0_attn_Wo"),
                     ("wfkT", "L0_attn_Wfk"), ("wfvT", "L0_attn_Wfv"),
                     ("wfrT", "L0_attn_Wfr")]:
        d[key] = np.ascontiguousarray(np.asarray(inputs[src], dtype=f32).T)
    d["AsT"] = np.ascontiguousarray(
        np.asarray(inputs["L0_ffn_lora_As"], dtype=f32).transpose(0, 2, 1).reshape(8, C * 16))
    d["BsT"] = np.ascontiguousarray(
        np.asarray(inputs["L0_ffn_lora_Bs"], dtype=f32).transpose(0, 2, 1).reshape(32, 16384))

    def v8(name):
        return np.ascontiguousarray(
            np.asarray(inputs[name], dtype=f32).reshape(8, 128).T)

    d["ln1w"] = v8("L0_attn_ln1_w")
    d["ln1b"] = v8("L0_attn_ln1_b")
    d["ln2w"] = v8("L0_attn_ln2_w")
    d["ln2b"] = v8("L0_attn_ln2_b")
    d["dec"] = v8("L0_attn_decay")
    d["uu"] = v8("L0_attn_u")
    d["tki"] = np.ascontiguousarray(
        np.asarray(inputs["L0_ffn_top_k_indices"], dtype=np.int32).reshape(2, 1))
    d["tkv"] = np.ascontiguousarray(
        np.asarray(inputs["L0_ffn_top_k_values"], dtype=f32).reshape(1, 2))
    d["onesf"] = np.ones((128, 1), f32)
    d["onesb"] = np.ones((128, 1), BF)
    d["onesr"] = np.ones((65, 128), f32)
    return d


def kernel(**inputs):
    x = np.asarray(inputs["L0_attn_x"], dtype=np.float32)
    shared = _prep_inputs(inputs)
    in_maps = []
    for c in range(NCORE):
        m = dict(shared)
        m["xT"] = np.ascontiguousarray(x[c * SEQ:(c + 1) * SEQ].transpose(0, 2, 1))
        in_maps.append(m)
    nc = build_program()
    res = run_bass_kernel_spmd(nc, in_maps, core_ids=list(range(NCORE)))
    outs = [np.asarray(r["outT"]).transpose(0, 2, 1) for r in res.results]
    return np.ascontiguousarray(np.concatenate(outs, axis=0)).astype(np.float32)
